# version 04
# baseline (speedup 1.0000x reference)
import os
os.environ["TRN_ALLOW_ICI_ALL_TO_ALL"] = "1"
import numpy as np
import ml_dtypes
import concourse.bass as bass
import concourse.mybir as mybir

F32 = mybir.dt.float32
BF16 = mybir.dt.bfloat16
AF = mybir.ActivationFunctionType
ALU = mybir.AluOpType
AX = mybir.AxisListType
BF = ml_dtypes.bfloat16

B, T, D = 2, 16384, 2048
H, DK, DV = 8, 256, 256
E, HID = 8, 1024
NC = 8
TW = 4096
NBLK = TW // 128
CH = 128
W13_ROWS = (2 * E + 2) * 8 * 128
W2_ROWS = (E + 1) * HID




TT = 1024
NTT = 16
NCH = 8
CH = 128
LN_DK = -2.772588722239781
NLEV = 6


def xi(proj, mb):
    return 2 * proj + mb


def build_p2_events():
    ev = []
    cnt = {}

    def bump(k, n=1):
        cnt[k] = cnt.get(k, 0) + n
        return cnt[k]

    def val(k):
        return cnt.get(k, 0)

    def E(kind, **kw):
        kw["kind"] = kind
        ev.append(kw)
        return kw

    guard = {"proj0": ("pevac", 0), "proj1": ("pevac", 0), "misc": ("l2r", 0),
             "bf_z0": ("dZ0", 0), "bf_bt": ("bevac", 0)}

    last_proj_tmm = 0
    for tt in range(NTT):
        for p in range(2):
            E("xnT", tt=tt, p=p, wait_tmm=(last_proj_tmm if (tt, p) != (0, 0) else None),
              xnt_total=bump("xnt", 16) * 16)
            e = E("beta", tt=tt, p=p, xnt_need=val("xnt") * 16,
                  misc_guard=guard["misc"], bt_guard=guard["bf_bt"],
                  bmm=[bump("tmm") for _ in range(2)],
                  bsig=[bump("bsig") for _ in range(2)],
                  bcast=bump("bcast"),
                  btr=[bump("tmm") for _ in range(NCH)],
                  bevac=bump("bevac"))
            guard["misc"] = ("bsig", e["bsig"][1])
            guard["bf_bt"] = ("bevac", e["bevac"])
            E("stash", tt=tt, p=p, estash=bump("estash"),
              cdone_need=val("cdone"), pevac_need=val("pevac"))
            for proj in range(3):
                for mb in range(2):
                    for ch in range(2):
                        bank = ch
                        e = E("proj", tt=tt, p=p, proj=proj, mb=mb, ch=ch, bank=bank,
                              xnt_need=val("xnt") * 16,
                              estash_need=val("estash"),
                              esave_need=val("esave"),
                              cdone_need=val("cdone"),
                              bank_guard=guard[f"proj{bank}"],
                              pmm=bump("tmm"),
                              pevac=bump("pevac"))
                        guard[f"proj{bank}"] = ("pevac", e["pevac"])
                        last_proj_tmm = e["pmm"]
            E("stash_save", tt=tt, p=p, pevac_need=val("pevac"), esave=bump("esave"))
            for proj in range(3):
                for mb in range(2):
                    E("conv", tt=tt, p=p, proj=proj, mb=mb,
                      pevac_need=val("pevac"), esave_need=val("esave"),
                      kvst_guard=val("kvst") * 16,
                      cdone=bump("cdone"), csil=bump("csil"))
            for which in range(2):
                lch = []
                e = {"kind": "l2", "tt": tt, "p": p, "which": which,
                     "csil_need": val("csil"), "kvst_guard": val("kvst") * 16,
                     "l2sq": bump("l2sq")}
                for ch in range(2):
                    d = {"misc_guard": guard["misc"],
                         "bank_guard": guard[f"proj{ch}"],
                         "l2mm": bump("tmm"),
                         "l2r": bump("l2r"),
                         "l2rb": bump("l2rb"),
                         "l2bc": bump("tmm")}
                    guard["misc"] = ("l2r", d["l2r"])
                    lch.append(d)
                e["lch"] = lch
                e["l2d"] = bump("l2d")
                for ch in range(2):
                    guard[f"proj{ch}"] = ("l2d", e["l2d"])
                ev.append(e)
            E("kvstore", tt=tt, p=p, l2d_need=val("l2d"), csil_need=val("csil"),
              kvst_total=bump("kvst", 4) * 16)
        for p in range(2):
            E("ktv", tt=tt, p=p, kvst_need=val("kvst") * 16,
              tmm_need=val("tmm"), dXL_need=(val("dXL0"), val("dXL1")),
              ktvl_total=bump("ktvl", 8) * 16)
            if tt == 0:
                E("sinit", p=p, sS=bump(f"sS{p}"))
        for i in range(NCH):
            for p in range(2):
                o = tt * NCH + i
                e = {"kind": "chunk", "tt": tt, "p": p, "i": i, "o": o,
                     "ktvl_need": val("ktvl") * 16,
                     "bevac_need": val("bevac"), "l2d_need": val("l2d"),
                     "dXL_prev": val(f"dXL{p}"), "z0_prev": val(f"dZ0{p}"),
                     "aXZ_prev": val(f"aXZ{p}"), "dPT_prev": val(f"dPT{p}"),
                     "dW_prev": val(f"dW{p}"), "dU_prev": val(f"dU{p}"),
                     "aO_prev": val(f"aO{p}"), "sS_need": val(f"sS{p}"),
                     "ds_guard": (val("sS0"), val("sS1")),
                     "misc_guard": guard["misc"],
                     "ost_prev": val("ost") * 16}
                guard["misc"] = (f"sS{p}", val(f"sS{p}") + 1)
                e["tmA"] = bump("tmm")
                e["dXL"] = bump(f"dXL{p}")
                e["tmZ0"] = bump("tmm")
                e["dZ0"] = bump(f"dZ0{p}")
                lev = []
                for kk in range(1, NLEV + 1):
                    lv = {"tmsq": bump("tmm"), "aXZ": bump(f"aXZ{p}"),
                          "tmpt": bump("tmm"), "dPT": bump(f"dPT{p}")}
                    lev.append(lv)
                e["lev"] = lev
                e["tmWU"] = bump("tmm")
                e["dW"] = bump(f"dW{p}")
                e["tmU"] = bump("tmm")
                e["dU"] = bump(f"dU{p}")
                e["tmO"] = bump("tmm")
                e["aO"] = bump(f"aO{p}")
                e["sS"] = bump(f"sS{p}")
                ev.append(e)
        guard["proj0"] = ("dW1", val("dW1"))
        guard["proj1"] = ("dPT1", val("dPT1"))
        for p in range(2):
            E("rms", tt=tt, p=p, aO_need=val(f"aO{p}"), tmm_need=val("tmm"),
              rms_z=bump("rms_z"),
              rsq=bump("rsq"),
              rms_ms=bump("rms_ms"), rms_r=bump("rms_r"),
              rnorm=bump("rnorm"),
              ost_total=bump("ost", NCH) * 16)
    E("ag2", ost_need=val("ost") * 16)
    return ev, cnt


P2_SEM_NAMES = ["xnt", "tmm", "bsig", "bcast", "bevac", "estash", "esave", "pevac",
                "cdone", "csil", "l2sq", "l2r", "l2rb", "l2d", "kvst", "ktvl",
                "dXL0", "dXL1", "dZ00", "dZ01", "aXZ0", "aXZ1", "dPT0", "dPT1", "dW0", "dW1", "dU0", "dU1", "aO0", "aO1", "sS0", "sS1",
                "rms_z", "rsq", "rms_ms", "rms_r", "rnorm", "ost"]


def p2_alloc(nc, c):
    A = nc.alloc_sbuf_tensor
    off = [0]

    def carve(cols):
        off[0] = (off[0] + 15) // 16 * 16
        ap = c.ar_bf[:, off[0]:off[0] + cols]
        off[0] += cols
        return ap
    c.xnT = carve(16 * TT)
    c.xhat = [carve(3 + TT) for i in range(6)]
    c.cv = [carve(TT) for i in range(6)]
    c.stashB = [A(f"stashB{i}", [128, 8], BF16) for i in range(6)]
    c.sqb = carve(2 * TT)
    c.KT = [carve(2 * TT) for p in range(2)]
    c.QT = [carve(2 * TT) for p in range(2)]
    c.ktv0 = carve(NCH * 512)
    assert off[0] <= 43120, off[0]
    c.brow = c.ar_f32[0:2, 0:TT]
    c.rs_row = c.ar_f32[0:1, TT:2 * TT]
    c.browb = A("browb", [2, TT], BF16)
    c.rs_bf = A("rs_bf", [1, TT], BF16)
    c.bcol = A("bcol", [128, 4 * NCH], F32)
    c.XS, c.X2, c.Z2, c.PT, c.LtS, c.BVK, c.negWT, c.Ub = {}, {}, {}, {}, {}, {}, {}, {}
    for p_ in range(2):
        b0 = 63600 + p_ * 2048
        c.XS[p_] = c.ar_bf[:, b0:b0 + 128]
        c.X2[p_] = [c.ar_bf[:, b0 + 128 + i * 128:b0 + 256 + i * 128] for i in range(2)]
        c.Z2[p_] = [c.ar_bf[:, b0 + 384 + i * 128:b0 + 512 + i * 128] for i in range(2)]
        c.PT[p_] = [c.ar_bf[:, b0 + 640 + i * 128:b0 + 768 + i * 128] for i in range(2)]
        c.LtS[p_] = c.ar_bf[:, b0 + 896:b0 + 1024]
        c.BVK[p_] = c.ar_bf[:, b0 + 1024:b0 + 1536]
        c.negWT[p_] = c.ar_bf[:, b0 + 1536:b0 + 1792]
        c.Ub[p_] = c.ar_bf[:, b0 + 1792:b0 + 2048]
    c.ktv1 = c.ar_bf[:, 57456:57456 + NCH * 512]
    c.ktv = [c.ktv0, c.ktv1]
    c.Otile1 = c.ar_bf[:, 61552:61552 + NCH * 256]
    c.S32 = c.ar_f32[:, 8192:9216]
    c.Sb = A("Sb", [128, 2 * 512], BF16)
    c.Otile = [c.ar_bf[:, 43120:43120 + NCH * 256], c.Otile1]
    c.rmsacc = A("rmsacc", [128, NCH], F32)
    c.rr = A("rr", [128, NCH], F32)
    P = nc.alloc_psum_tensor
    c.pb_proj = [P(f"pb_proj{i}", [128, 512], F32) for i in range(2)]
    c.pb_alw = P("pb_alw", [128, 512], F32)
    c.pb_ser = P("pb_ser", [128, 384], F32)
    c.pb_uo = P("pb_uo", [128, 512], F32)
    c.pb_ds = P("pb_ds", [128, 512], F32)
    c.pb_bf = P("pb_bf", [128, 1024], BF16)
    c.pb_misc = P("pb_misc", [128, 512], F32)


def _pb(c, p):
    if p == 0:
        return c.pb_alw, c.pb_ser, c.pb_uo, (0, 128)
    return c.pb_proj[0], c.pb_proj[1][:, 0:384], c.pb_ds, (384, 512)


def _kq(c, p, which, mb, i):
    buf = c.QT[p] if which == 0 else c.KT[p]
    return buf[:, mb * TT + i * CH: mb * TT + (i + 1) * CH]


def _g(S, eng, g):
    sem_name, v = g
    if v > 0:
        eng.wait_ge(S[sem_name], v)


def p2_sync(c, s, nc, evs):
    S = c.sems
    for e in evs:
        k = e["kind"]
        if k == "xnT":
            if e["wait_tmm"] is not None:
                s.wait_ge(S["tmm"], e["wait_tmm"])
            tt, p = e["tt"], e["p"]
            base = p * 16384 + tt * TT
            for db in range(16):
                s.dma_start(out=c.xnT[:, db * TT:(db + 1) * TT],
                            in_=c.ag1_out[base:base + TT, db * 128:(db + 1) * 128],
                            transpose=True).then_inc(S["xnt"], 16)
        elif k == "ktv":
            tt, p = e["tt"], e["p"]
            s.wait_ge(S["kvst"], e["kvst_need"])
            s.wait_ge(S["tmm"], e["tmm_need"])
            s.wait_ge(S["dXL0"], e["dXL_need"][0])
            s.wait_ge(S["dXL1"], e["dXL_need"][1])
            for i in range(NCH):
                s.dma_start(out=c.ktv[p][:, i * 512:(i + 1) * 512],
                            in_=c.kv_scr[p * 512:(p + 1) * 512,
                                         tt * TT + i * CH: tt * TT + (i + 1) * CH],
                            transpose=True).then_inc(S["ktvl"], 16)


def p2_tensor(c, t, nc, evs):
    S = c.sems
    t.wait_ge(S["p0_cast"], 64)
    t.wait_ge(S["p0_consts"], 16 * 9)
    for e in evs:
        k = e["kind"]
        if k == "beta":
            t.wait_ge(S["xnt"], e["xnt_need"])
            _g(S, t, e["misc_guard"])
            for ch in range(2):
                for db in range(16):
                    mm = t.matmul(c.pb_misc[0:2, 0:512],
                                  c.wbS[:, db * 2:(db + 1) * 2],
                                  c.xnT[:, db * TT + ch * 512: db * TT + (ch + 1) * 512],
                                  start=(db == 0), stop=(db == 15))
                mm.then_inc(S["tmm"], 1)
                if ch == 0:
                    t.wait_ge(S["bsig"], e["bsig"][0])
            t.wait_ge(S["bcast"], e["bcast"])
            _g(S, t, e["bt_guard"])
            for i in range(NCH):
                t.transpose(c.pb_bf[:, 128 + 2 * i:128 + 2 * (i + 1)],
                            c.browb[0:2, i * CH:(i + 1) * CH],
                            c.identS[0:2, 0:2]).then_inc(S["tmm"], 1)
        elif k == "proj":
            p, proj, mb, ch = e["p"], e["proj"], e["mb"], e["ch"]
            t.wait_ge(S["xnt"], e["xnt_need"])
            t.wait_ge(S["estash"], e["estash_need"])
            _g(S, t, e["bank_guard"])
            wS = [c.wqS, c.wkS, c.wvS][proj]
            psum = c.pb_proj[e["bank"]]
            for db in range(16):
                mm = t.matmul(psum[:, 0:512],
                              wS[:, db * 256 + mb * 128: db * 256 + mb * 128 + 128],
                              c.xnT[:, db * TT + ch * 512: db * TT + (ch + 1) * 512],
                              start=(db == 0), stop=(db == 15))
            mm.then_inc(S["tmm"], 1)
        elif k == "l2":
            p, which = e["p"], e["which"]
            t.wait_ge(S["l2sq"], e["l2sq"])
            for ch in range(2):
                d = e["lch"][ch]
                _g(S, t, d["misc_guard"])
                for mb in range(2):
                    mm = t.matmul(c.pb_misc[0:1, 0:512], c.onecS[:],
                                  c.sqb[:, mb * TT + ch * 512: mb * TT + (ch + 1) * 512],
                                  start=(mb == 0), stop=(mb == 1))
                mm.then_inc(S["tmm"], 1)
                t.wait_ge(S["l2rb"], d["l2rb"])
                _g(S, t, d["bank_guard"])
                t.matmul(c.pb_proj[ch][:, 0:512], c.onerS[:],
                         c.rs_bf[0:1, ch * 512:(ch + 1) * 512],
                         start=True, stop=True).then_inc(S["tmm"], 1)
        elif k == "chunk":
            p, i = e["p"], e["i"]
            AL, SER, UO, z0r = _pb(c, p)
            t.wait_ge(S["ktvl"], e["ktvl_need"])
            t.wait_ge(S["l2d"], e["l2d_need"])
            t.wait_ge(S[f"dXL{p}"], e["dXL_prev"])
            for mb in range(2):
                t.matmul(AL[:, 0:128], _kq(c, p, 1, mb, i), _kq(c, p, 1, mb, i),
                         start=(mb == 0), stop=(mb == 1))
            for mb in range(2):
                mm = t.matmul(AL[:, 128:256], _kq(c, p, 1, mb, i), _kq(c, p, 0, mb, i),
                              start=(mb == 0), stop=(mb == 1))
            mm.then_inc(S["tmm"], 1)
            t.wait_ge(S[f"dXL{p}"], e["dXL"])
            t.wait_ge(S[f"dZ0{p}"], e["z0_prev"])
            t.transpose(c.pb_bf[:, z0r[0]:z0r[1]], c.XS[p][:], c.identS[:]).then_inc(S["tmm"], 1)
            t.wait_ge(S[f"dZ0{p}"], e["dZ0"])
            t.wait_ge(S[f"aXZ{p}"], e["aXZ_prev"])
            t.wait_ge(S[f"dPT{p}"], e["dPT_prev"])
            Xcur, Zcur = c.XS[p][:], c.Z2[p][0][:]
            for lv in range(1, NLEV + 1):
                d = e["lev"][lv - 1]
                if lv >= 2:
                    t.wait_ge(S[f"aXZ{p}"], e["lev"][lv - 2]["aXZ"])
                    t.wait_ge(S[f"dPT{p}"], e["lev"][lv - 2]["dPT"])
                mm = t.matmul(SER[:, 0:128], Zcur, Xcur, start=True, stop=True)
                if lv < NLEV:
                    mm = t.matmul(SER[:, 128:256], Xcur, Zcur, start=True, stop=True)
                mm.then_inc(S["tmm"], 1)
                t.wait_ge(S[f"aXZ{p}"], d["aXZ"])
                t.matmul(SER[:, 256:384], c.X2[p][lv % 2][:], c.PT[p][(lv - 1) % 2][:],
                         start=True, stop=True).then_inc(S["tmm"], 1)
                Xcur, Zcur = c.X2[p][lv % 2][:], c.Z2[p][lv % 2][:]
            t.wait_ge(S[f"dPT{p}"], e["lev"][NLEV - 1]["dPT"])
            t.wait_ge(S[f"dW{p}"], e["dW_prev"])
            t.wait_ge(S[f"dU{p}"], e["dU_prev"])
            PTf = c.PT[p][NLEV % 2]
            for mb in range(2):
                t.matmul(AL[:, 256 + mb * 128:256 + (mb + 1) * 128],
                         c.BVK[p][:, mb * 128:(mb + 1) * 128], PTf[:], start=True, stop=True)
            mm = t.matmul(UO[:, 0:256], PTf[:], c.BVK[p][:, 256:512], start=True, stop=False)
            mm.then_inc(S["tmm"], 1)
            t.wait_ge(S[f"dW{p}"], e["dW"])
            t.wait_ge(S[f"sS{p}"], e["sS_need"])
            for mb in range(2):
                mm = t.matmul(UO[:, 0:256], c.negWT[p][:, mb * 128:(mb + 1) * 128],
                              c.Sb[:, p * 512 + mb * 256: p * 512 + (mb + 1) * 256],
                              start=False, stop=(mb == 1))
            mm.then_inc(S["tmm"], 1)
            t.wait_ge(S[f"dU{p}"], e["dU"])
            t.wait_ge(S[f"aO{p}"], e["aO_prev"])
            t.wait_ge(S["sS0"], e["ds_guard"][0])
            t.wait_ge(S["sS1"], e["ds_guard"][1])
            _g(S, t, e["misc_guard"]) if isinstance(e["misc_guard"][1], int) and e["misc_guard"][1] > 0 and e["misc_guard"][0] in ("l2r", "bsig") else None
            for mb in range(2):
                t.matmul(UO[:, 256:512], _kq(c, p, 0, mb, i),
                         c.Sb[:, p * 512 + mb * 256: p * 512 + (mb + 1) * 256],
                         start=(mb == 0), stop=False)
            t.matmul(UO[:, 256:512], c.LtS[p][:], c.Ub[p][:], start=False, stop=True)
            for mb in range(2):
                mm = t.matmul(c.pb_misc[:, mb * 256:(mb + 1) * 256],
                              c.ktv[p][:, i * 512 + mb * 128: i * 512 + (mb + 1) * 128],
                              c.Ub[p][:], start=True, stop=True)
            mm.then_inc(S["tmm"], 1)


def p2_scalar(c, sc, nc, evs, act_inc):
    S = c.sems
    for e in evs:
        k = e["kind"]
        if k == "beta":
            for ch in range(2):
                sc.wait_ge(S["tmm"], e["bmm"][ch])
                sc.activation(out=c.brow[0:2, ch * 512:(ch + 1) * 512],
                              in_=c.pb_misc[0:2, 0:512], func=AF.Sigmoid)
                act_inc(c, sc, S["bsig"])
        elif k == "proj":
            proj, mb, ch = e["proj"], e["mb"], e["ch"]
            sc.wait_ge(S["tmm"], e["pmm"])
            sc.wait_ge(S["esave"], e["esave_need"])
            sc.wait_ge(S["cdone"], e["cdone_need"])
            sc.activation(out=c.xhat[xi(proj, mb)][:, 3 + ch * 512: 3 + (ch + 1) * 512],
                          in_=c.pb_proj[e["bank"]][:, 0:512], func=AF.Copy)
            act_inc(c, sc, S["pevac"])
        elif k == "conv":
            sc.wait_ge(S["cdone"], e["cdone"])
            buf = c.cv[xi(e["proj"], e["mb"])]
            sc.activation(out=buf[:], in_=buf[:], func=AF.Silu)
            act_inc(c, sc, S["csil"])
        elif k == "l2":
            which = e["which"]
            for ch in range(2):
                d = e["lch"][ch]
                sc.wait_ge(S["tmm"], d["l2mm"])
                sc.activation(out=c.rs_row[0:1, ch * 512:(ch + 1) * 512],
                              in_=c.pb_misc[0:1, 0:512], func=AF.Ln, bias=1e-6)
                sc.drain()
                if which == 0:
                    sc.activation(out=c.rs_row[0:1, ch * 512:(ch + 1) * 512],
                                  in_=c.rs_row[0:1, ch * 512:(ch + 1) * 512],
                                  func=AF.Exp, scale=-0.5, bias=LN_DK)
                else:
                    sc.activation(out=c.rs_row[0:1, ch * 512:(ch + 1) * 512],
                                  in_=c.rs_row[0:1, ch * 512:(ch + 1) * 512],
                                  func=AF.Exp, scale=-0.5)
                act_inc(c, sc, S["l2r"])
        elif k == "chunk":
            p = e["p"]
            AL, SER, UO, z0r = _pb(c, p)
            for lv in range(1, NLEV + 1):
                d = e["lev"][lv - 1]
                sc.wait_ge(S["tmm"], d["tmsq"])
                sc.activation(out=c.X2[p][lv % 2][:], in_=SER[:, 0:128], func=AF.Copy)
                if lv < NLEV:
                    sc.drain()
                    sc.activation(out=c.Z2[p][lv % 2][:], in_=SER[:, 128:256], func=AF.Copy)
                act_inc(c, sc, S[f"aXZ{p}"])
            sc.wait_ge(S["tmm"], e["tmO"])
            sc.wait_ge(S["ost"], e["ost_prev"])
            sc.activation(out=c.Otile[p][:, e["i"] * 256:(e["i"] + 1) * 256],
                          in_=UO[:, 256:512], func=AF.Copy)
            act_inc(c, sc, S[f"aO{p}"])
        elif k == "rms":
            sc.wait_ge(S[f"aO{e['p']}"], e["aO_need"])
            sc.wait_ge(S["tmm"], e["tmm_need"])
            sc.wait_ge(S["rms_z"], e["rms_z"])
            for i in range(NCH):
                sc.activation(out=c.sqb[:, 0:256], in_=c.Otile[e["p"]][:, i * 256:(i + 1) * 256],
                              func=AF.Square, accum_out=c.rmsacc[:, i:i + 1])
                sc.drain()
            act_inc(c, sc, S["rsq"])
            sc.wait_ge(S["rms_ms"], e["rms_ms"])
            sc.activation(out=c.rr[:], in_=c.rr[:], func=AF.Ln, bias=1e-5)
            sc.drain()
            sc.activation(out=c.rr[:], in_=c.rr[:], func=AF.Exp, scale=-0.5)
            act_inc(c, sc, S["rms_r"])


def p2_vector(c, v, nc, evs, dve_inc):
    S = c.sems
    for e in evs:
        k = e["kind"]
        if k == "beta":
            v.wait_ge(S["bsig"], e["bsig"][1])
            v.tensor_copy(out=c.browb[0:2, :], in_=c.brow[0:2, :])
            dve_inc(v, S["bcast"])
            v.wait_ge(S["tmm"], e["btr"][NCH - 1])
            v.tensor_copy(out=c.bcol[:, e["p"] * 16:e["p"] * 16 + 16],
                          in_=c.pb_bf[:, 128:128 + 2 * NCH])
            dve_inc(v, S["bevac"])
        elif k == "stash":
            tt, p = e["tt"], e["p"]
            v.wait_ge(S["cdone"], e["cdone_need"])
            v.wait_ge(S["pevac"], e["pevac_need"])
            for b in range(6):
                if tt == 0 and p == 0:
                    v.memset(c.stashB[b][:], 0.0)
                    v.drain()
                v.tensor_copy(out=c.xhat[b][:, 0:3], in_=c.stashB[b][:, p * 4:p * 4 + 3])
            dve_inc(v, S["estash"])
        elif k == "stash_save":
            p = e["p"]
            v.wait_ge(S["pevac"], e["pevac_need"])
            for b in range(6):
                v.tensor_copy(out=c.stashB[b][:, p * 4:p * 4 + 3], in_=c.xhat[b][:, TT:TT + 3])
            dve_inc(v, S["esave"])
        elif k == "conv":
            p, proj, mb = e["p"], e["proj"], e["mb"]
            v.wait_ge(S["pevac"], e["pevac_need"])
            v.wait_ge(S["esave"], e["esave_need"])
            v.wait_ge(S["kvst"], e["kvst_guard"])
            xb = c.xhat[xi(proj, mb)]
            out = c.cv[xi(proj, mb)]
            base = proj * 16 + (p * 2 + mb) * 4
            v.tensor_scalar(out=out[:], in0=xb[:, 3:3 + TT],
                            scalar1=c.convS[:, base + 3:base + 4], scalar2=None, op0=ALU.mult)
            for j in (2, 1, 0):
                v.drain()
                v.scalar_tensor_tensor(out=out[:], in0=xb[:, j:j + TT],
                                       scalar=c.convS[:, base + j:base + j + 1],
                                       in1=out[:], op0=ALU.mult, op1=ALU.add)
            dve_inc(v, S["cdone"])
        elif k == "l2":
            p, which = e["p"], e["which"]
            v.wait_ge(S["csil"], e["csil_need"])
            for mb in range(2):
                buf = c.cv[xi(which, mb)]
                v.tensor_mul(out=c.sqb[:, mb * TT:(mb + 1) * TT], in0=buf[:], in1=buf[:])
                v.drain()
            dve_inc(v, S["l2sq"])
            dst = (c.QT if which == 0 else c.KT)[p]
            for ch in range(2):
                d = e["lch"][ch]
                v.wait_ge(S["l2r"], d["l2r"])
                v.tensor_copy(out=c.rs_bf[0:1, ch * 512:(ch + 1) * 512],
                              in_=c.rs_row[0:1, ch * 512:(ch + 1) * 512])
                dve_inc(v, S["l2rb"])
                v.wait_ge(S["tmm"], d["l2bc"])
                v.wait_ge(S["kvst"], e["kvst_guard"])
                for mb in range(2):
                    v.tensor_mul(out=dst[:, mb * TT + ch * 512: mb * TT + (ch + 1) * 512],
                                 in0=c.cv[xi(which, mb)][:, ch * 512:(ch + 1) * 512],
                                 in1=c.pb_proj[ch][:, 0:512])
                    v.drain()
            dve_inc(v, S["l2d"])
        elif k == "sinit":
            p = e["p"]
            v.memset(c.S32[:, p * 512:(p + 1) * 512], 0.0)
            v.memset(c.Sb[:, p * 512:(p + 1) * 512], 0.0)
            dve_inc(v, S[f"sS{p}"])
        elif k == "chunk":
            p, i = e["p"], e["i"]
            AL, SER, UO, z0r = _pb(c, p)
            v.wait_ge(S["tmm"], e["tmA"])
            v.wait_ge(S["ktvl"], e["ktvl_need"])
            v.wait_ge(S["bevac"], e["bevac_need"])
            v.scalar_tensor_tensor(out=c.XS[p][:], in0=AL[:, 0:128],
                                   scalar=c.bcol[:, p * 16 + i * 2: p * 16 + i * 2 + 1],
                                   in1=c.negstrilS[:], op0=ALU.mult, op1=ALU.mult)
            v.drain()
            v.tensor_mul(out=c.LtS[p][:], in0=AL[:, 128:256], in1=c.triuS[:])
            v.drain()
            v.tensor_scalar(out=c.BVK[p][:], in0=c.ktv[p][:, i * 512:(i + 1) * 512],
                            scalar1=c.bcol[:, p * 16 + i * 2: p * 16 + i * 2 + 1], scalar2=None,
                            op0=ALU.mult)
            dve_inc(v, S[f"dXL{p}"])
            v.wait_ge(S["tmm"], e["tmZ0"])
            v.tensor_copy(out=c.Z2[p][0][:], in_=c.pb_bf[:, z0r[0]:z0r[1]])
            v.drain()
            v.tensor_add(out=c.PT[p][0][:], in0=c.Z2[p][0][:], in1=c.identS[:])
            dve_inc(v, S[f"dZ0{p}"])
            for lv in range(1, NLEV + 1):
                d = e["lev"][lv - 1]
                v.wait_ge(S["tmm"], d["tmpt"])
                v.tensor_add(out=c.PT[p][lv % 2][:], in0=c.PT[p][(lv - 1) % 2][:],
                             in1=SER[:, 256:384])
                dve_inc(v, S[f"dPT{p}"])
            v.wait_ge(S["tmm"], e["tmWU"])
            v.tensor_scalar(out=c.negWT[p][:], in0=AL[:, 256:512], scalar1=-1.0,
                            scalar2=None, op0=ALU.mult)
            dve_inc(v, S[f"dW{p}"])
            v.wait_ge(S["tmm"], e["tmU"])
            v.tensor_copy(out=c.Ub[p][:], in_=UO[:, 0:256])
            dve_inc(v, S[f"dU{p}"])
            v.wait_ge(S["tmm"], e["tmO"])
            for mb in range(2):
                v.tensor_add(out=c.S32[:, p * 512 + mb * 256: p * 512 + (mb + 1) * 256],
                             in0=c.S32[:, p * 512 + mb * 256: p * 512 + (mb + 1) * 256],
                             in1=c.pb_misc[:, mb * 256:(mb + 1) * 256])
                v.drain()
            v.tensor_copy(out=c.Sb[:, p * 512:(p + 1) * 512], in_=c.S32[:, p * 512:(p + 1) * 512])
            dve_inc(v, S[f"sS{p}"])
            if False:
                v.drain()
                v.tensor_copy(out=c.dbgC[:, 0:128], in_=c.XS[:])
                v.tensor_copy(out=c.dbgC[:, 128:256], in_=c.LtS[:])
                v.tensor_copy(out=c.dbgC[:, 256:384], in_=c.PT[NLEV % 2][:])
                v.tensor_copy(out=c.dbgC[:, 384:896], in_=c.BVK[:])
                v.tensor_copy(out=c.dbgC[:, 896:1152], in_=c.Ub[:])
                v.tensor_copy(out=c.dbgC[:, 1152:1280], in_=c.QT[0][:, 0:128])
                v.tensor_copy(out=c.dbgC[:, 1280:1408], in_=c.QT[0][:, TT:TT + 128])
                dve_inc(v, S["dbgc"])
        elif k == "rms":
            v.memset(c.rmsacc[:], 0.0)
            dve_inc(v, S["rms_z"])
            v.wait_ge(S["rsq"], e["rsq"])
            v.tensor_scalar(out=c.rr[:], in0=c.rmsacc[:], scalar1=1.0 / 256, scalar2=None,
                            op0=ALU.mult)
            dve_inc(v, S["rms_ms"])
            v.wait_ge(S["rms_r"], e["rms_r"])
            for i in range(NCH):
                v.scalar_tensor_tensor(out=c.Otile[e["p"]][:, i * 256:(i + 1) * 256],
                                       in0=c.Otile[e["p"]][:, i * 256:(i + 1) * 256],
                                       scalar=c.rr[:, i:i + 1], in1=c.normwS[:],
                                       op0=ALU.mult, op1=ALU.mult)
                v.drain()
            dve_inc(v, S["rnorm"])


def p2_gpsimd(c, g, nc, evs, ag_groups):
    S = c.sems
    for e in evs:
        k = e["kind"]
        if k == "kvstore":
            tt, p = e["tt"], e["p"]
            g.wait_ge(S["l2d"], e["l2d_need"])
            g.wait_ge(S["csil"], e["csil_need"])
            for mb in range(2):
                g.dma_start(out=c.kv_scr[p * 512 + mb * 128: p * 512 + (mb + 1) * 128,
                                         tt * TT:(tt + 1) * TT],
                            in_=c.KT[p][:, mb * TT:(mb + 1) * TT]).then_inc(S["kvst"], 16)
                g.dma_start(out=c.kv_scr[p * 512 + 256 + mb * 128: p * 512 + 256 + (mb + 1) * 128,
                                         tt * TT:(tt + 1) * TT],
                            in_=c.cv[xi(2, mb)][:]).then_inc(S["kvst"], 16)
        elif k == "rms":
            tt, p = e["tt"], e["p"]
            g.wait_ge(S["rnorm"], e["rnorm"])
            for i in range(NCH):
                t_ = tt * TT + i * CH
                row = (p * 4 + t_ // 4096) * 4096 + (t_ % 4096)
                g.dma_start(out=c.ag2_in[row:row + CH, 0:256],
                            in_=c.Otile[e["p"]][:, i * 256:(i + 1) * 256]).then_inc(S["ost"], 16)
        elif k == "ag2":
            g.wait_ge(S["ost"], e["ost_need"])
            from concourse.aot_env import allow_ici_all_to_all
            with allow_ici_all_to_all():
                g.collective_compute("AllToAll", ALU.bypass, replica_groups=ag_groups,
                                     ins=[c.ag2_in.ap()], outs=[c.ag2_out.ap()]).then_inc(S["cc"])



GRP = 512
NG = 8
NTB = 4


def build_p3_events():
    ev = []
    cnt = {}

    def bump(k, n=1):
        cnt[k] = cnt.get(k, 0) + n
        return cnt[k]

    def val(k):
        return cnt.get(k, 0)

    def E(kind, **kw):
        kw["kind"] = kind
        ev.append(kw)
        return kw

    w13c = [0, 0, 0, 0]
    w13_guard = [0, 0, 0, 0]
    w2c = [0, 0]
    w2_guard = [0, 0]
    x1c = [0, 0]
    outc = [0, 0]
    hist = [0, 0]

    E("woload", wo_total=bump("p3wo", 16) * 16)
    for g in range(NG):
        E("oT", g=g, tmm_guard=val("tmm3"), ln2_guard=val("aLn"),
          hist_guard=(hist[0] * 48, hist[1] * 48),
          oTl=bump("oTl", 16) * 16, xld=bump("xld", 4) * 16)
        for tb in range(NTB):
            for dchunk in range(4):
                E("wo", g=g, tb=tb, dchunk=dchunk, bank=dchunk % 2,
                  oTl_need=val("oTl") * 16, wo_need=val("p3wo") * 16,
                  xld_need=val("xld") * 16,
                  bank_guard=max(val("aWo") - 1, 0),
                  tmm=bump("tmm3"), aWo=bump("aWo"))
            par = (g * NTB + tb) % 2
            e = E("ln2", g=g, tb=tb, par=par, aWo_need=val("aWo"),
                  hist_guard=hist[par] * 48,
                  sq=bump("sq3"), dv=bump("dv3"), st1=bump("st3"),
                  aLn=bump("aLn"))
            hist[par] += 1
            e["hist_total"] = hist[par] * 48
    for g in range(NG):
        E("xfT", g=g, hist_need=(hist[0] * 48, hist[1] * 48),
          tmm_guard=val("tmm3"), rin_guard=val("rin"), gate_guard=val("tmm3"),
          xfl=bump("xfl", 32) * 16)
        for tb in range(NTB):
            E("gate", g=g, tb=tb, xfl_need=val("xfl") * 16,
              gpsum_guard=val("aSig"), rin_guard=val("rin"),
              mskp_guard=val("mskst") * 16,
              tmm=bump("tmm3"), aSig=bump("aSig"), gdone=bump("gdone"),
              tmsk=bump("tmm3"), mskd=bump("mskd"),
              mskst=bump("mskst") * 16)
        rounds = [list(range(4)), list(range(4, 8)), [8]]
        for r, exps in enumerate(rounds):
            for e_ in exps:
                if e_ < 8:
                    E("mskb", g=g, e=e_, mskst_need=val("mskst") * 16,
                      rin_guard=val("rin"), aSig_guard=val("aSig"),
                      mskrl=bump("mskrl") * 16,
                      tmm=bump("tmm3"), rin=bump("rin"))
                for hb in range(8):
                    s1 = (val("w13ev") * 2) % 4
                    s3 = s1 + 1
                    bump("w13ev")
                    w13c[s1] += 1
                    w13c[s3] += 1
                    d = E("ffnh", g=g, e=e_, hb=hb, s1=s1, s3=s3,
                          rin_need=val("rin"),
                          w1_need=w13c[s1] * 16, w3_need=w13c[s3] * 16,
                          slot_guard=(w13_guard[s1], w13_guard[s3]),
                          aSil_guard=val("aSil"), hdone_guard=val("hdone"),
                          ydone_guard=val("ydone"),
                          tmm_h1=bump("tmm3"), aSil=bump("aSil"),
                          tmm_h3=bump("tmm3"), hdone=bump("hdone"))
                    w13_guard[s1] = d["tmm_h1"]
                    w13_guard[s3] = d["tmm_h3"]
            for dchunk in range(4):
                for si, e_ in enumerate(exps):
                    sl = val("w2ev") % 2
                    bump("w2ev")
                    w2c[sl] += 8
                    d = E("ffny", g=g, r=r, dchunk=dchunk, si=si, e=e_, slot=sl,
                          nsub=len(exps),
                          hdone_need=val("hdone"),
                          w2_need=w2c[sl] * 16, slot_guard=w2_guard[sl],
                          y_guard=val("ydone"),
                          tmm=bump("tmm3"),
                          first_round=(r == 0))
                    w2_guard[sl] = d["tmm"]
                    if si == len(exps) - 1:
                        d["ydone"] = bump("ydone")
                    else:
                        d["ydone"] = None
        for tb in range(NTB):
            buf = (g * NTB + tb) % 2
            x1c[buf] += 1
            e = E("fout", g=g, tb=tb, buf=buf, ydone_need=val("ydone"),
                  out_guard=outc[buf] * 16,
                  x1_need=x1c[buf] * 16,
                  odone=bump("odone"))
            outc[buf] += 1
            e["out_total"] = outc[buf] * 16
    E("p3fin", out_totals=(outc[0] * 16, outc[1] * 16))
    return ev, cnt


P3_SEM_NAMES = ["p3wo", "oTl", "xld", "tmm3", "aWo", "sq3", "dv3", "st3", "aLn",
                "histn0", "histn1", "xfl", "aSig", "gdone", "mskd", "rin",
                "w13l0", "w13l1", "w13l2", "w13l3", "aSil", "hdone", "w2l0", "w2l1",
                "ydone", "x1l0", "x1l1", "odone", "outst0", "outst1", "mskst", "mskrl"]


def p3_alloc(nc, c):
    bf = c.ar_bf
    f32 = c.ar_f32
    c.woALL = bf[:, 0:32768]
    c.oTt = bf[:, 32768:40960]
    c.hi_stg = [bf[:, 40960 + i * 2048:40960 + (i + 1) * 2048] for i in range(2)]
    c.lo_stg = [bf[:, 45056 + i * 2048:45056 + (i + 1) * 2048] for i in range(2)]
    c.sqj3 = bf[:, 49152:51200]
    c.x_g = f32[:, 0:8192]
    c.xf3 = f32[:, 8192:10240]
    c.hiT = bf[:, 0:8192]
    c.loT = bf[:, 8192:16384]
    c.rinT = bf[:, 8192:16384]
    c.hT = [bf[:, 16384 + i * 4096:16384 + (i + 1) * 4096] for i in range(4)]
    c.w13buf = [bf[:, 32768 + i * 2048:32768 + (i + 1) * 2048] for i in range(4)]
    c.w2D = [bf[:, 40960 + i * 4096:40960 + (i + 1) * 4096] for i in range(2)]
    c.hsil = bf[:, 49152:49664]
    c.y_acc = bf[:, 49664:57856]
    c.mskT8 = bf[0:8, 57856:57984]
    c.mskrow = bf[0:1, 57984:58496]
    c.mskbc = bf[:, 61952:62464]
    c.mskb8 = bf[:, 62464:62472]
    c.x1b = [f32[:, i * 2048:(i + 1) * 2048] for i in range(2)]
    c.sc8 = [f32[:, 4096 + i * 16:4096 + i * 16 + 8] for i in range(6)]
    c.m2t = f32[:, 4192:4200]
    c.r1t = f32[:, 4208:4216]


def w13_block(e, wtype):
    if e == 8:
        return 16 if wtype == 0 else 17
    return 2 * e + wtype


def p3_sync(c, s, nc, evs):
    S = c.sems
    s.wait_ge(S["cc"], 4)
    for e in evs:
        k = e["kind"]
        if k == "woload":
            for db in range(16):
                s.dma_start(out=c.woALL[:, db * 2048:(db + 1) * 2048],
                            in_=c.wo_scr[db * 128:(db + 1) * 128, :]).then_inc(S["p3wo"], 16)
        elif k == "oT":
            g = e["g"]
            s.wait_ge(S["tmm3"], e["tmm_guard"])
            s.wait_ge(S["aLn"], e["ln2_guard"])
            s.wait_ge(S["histn0"], e["hist_guard"][0])
            s.wait_ge(S["histn1"], e["hist_guard"][1])
            t0 = g * GRP
            for db in range(16):
                h = db // 2
                s.dma_start(out=c.oTt[:, db * 512:(db + 1) * 512],
                            in_=c.ag2_out[h * 4096 + t0:h * 4096 + t0 + GRP,
                                          (db % 2) * 128:(db % 2) * 128 + 128],
                            transpose=True).then_inc(S["oTl"], 16)
            for tb in range(NTB):
                s.dma_start(out=c.x_g[:, tb * 2048:(tb + 1) * 2048],
                            in_=c.x[t0 + tb * 128:t0 + (tb + 1) * 128, :]).then_inc(S["xld"], 16)
        elif k == "xfT":
            g = e["g"]
            s.wait_ge(S["histn0"], e["hist_need"][0])
            s.wait_ge(S["histn1"], e["hist_need"][1])
            s.wait_ge(S["tmm3"], e["tmm_guard"])
            s.wait_ge(S["rin"], e["rin_guard"])
            t0 = g * GRP
            for db in range(16):
                s.dma_start(out=c.hiT[:, db * 512:(db + 1) * 512],
                            in_=c.xfhi_scr[t0:t0 + GRP, db * 128:(db + 1) * 128],
                            transpose=True).then_inc(S["xfl"], 16)
            for db in range(16):
                s.dma_start(out=c.loT[:, db * 512:(db + 1) * 512],
                            in_=c.xflo_scr[t0:t0 + GRP, db * 128:(db + 1) * 128],
                            transpose=True).then_inc(S["xfl"], 16)
        elif k == "mskb":
            s.wait_ge(S["mskst"], e["mskst_need"])
            s.wait_ge(S["rin"], e["rin_guard"])
            s.dma_start(out=c.mskrow[0:1, :],
                        in_=c.msk_scr[e["e"]:e["e"] + 1, e["g"] * GRP:(e["g"] + 1) * GRP]
                        ).then_inc(S["mskrl"], 16)
        elif k == "ffnh":
            s.wait_ge(S["tmm3"], max(e["slot_guard"]))
            b1 = w13_block(e["e"], 0)
            b3 = w13_block(e["e"], 1)
            hb = e["hb"]
            s.dma_start(out=c.w13buf[e["s1"]][:],
                        in_=c.agw13_out[(b1 * 8 + hb) * 128:(b1 * 8 + hb + 1) * 128, :]
                        ).then_inc(S[f"w13l{e['s1']}"], 16)
            s.dma_start(out=c.w13buf[e["s3"]][:],
                        in_=c.agw13_out[(b3 * 8 + hb) * 128:(b3 * 8 + hb + 1) * 128, :]
                        ).then_inc(S[f"w13l{e['s3']}"], 16)
        elif k == "ffny":
            s.wait_ge(S["tmm3"], e["slot_guard"])
            base = e["e"] * 1024
            dchunk = e["dchunk"]
            for hb in range(8):
                s.dma_start(out=c.w2D[e["slot"]][:, hb * 512:(hb + 1) * 512],
                            in_=c.agw2_out[base + hb * 128:base + (hb + 1) * 128,
                                           dchunk * 512:(dchunk + 1) * 512]
                            ).then_inc(S[f"w2l{e['slot']}"], 16)
        elif k == "fout":
            g, tb, buf = e["g"], e["tb"], e["buf"]
            s.wait_ge(S["ydone"], e["ydone_need"])
            s.wait_ge(S[f"outst{buf}"], e["out_guard"])
            s.dma_start(out=c.x1b[buf][:],
                        in_=c.x1_scr[g * GRP + tb * 128:g * GRP + (tb + 1) * 128, :]
                        ).then_inc(S[f"x1l{buf}"], 16)


def p3_tensor(c, t, nc, evs):
    S = c.sems
    t.wait_ge(S["cc"], 4)
    t.wait_ge(S["p0_cast"], 80)
    psy = [c.pb_proj[0], c.pb_proj[1], c.pb_uo, c.pb_ds]
    for e in evs:
        k = e["kind"]
        if k == "wo":
            t.wait_ge(S["oTl"], e["oTl_need"])
            t.wait_ge(S["p3wo"], e["wo_need"])
            t.wait_ge(S["aWo"], e["bank_guard"])
            tb, dchunk = e["tb"], e["dchunk"]
            psum = c.pb_proj[e["bank"]]
            for db in range(16):
                mm = t.matmul(psum[:, 0:512],
                              c.oTt[:, db * 512 + tb * 128: db * 512 + tb * 128 + 128],
                              c.woALL[:, db * 2048 + dchunk * 512: db * 2048 + (dchunk + 1) * 512],
                              start=(db == 0), stop=(db == 15))
            mm.then_inc(S["tmm3"], 1)
        elif k == "gate":
            tb = e["tb"]
            t.wait_ge(S["xfl"], e["xfl_need"])
            t.wait_ge(S["aSig"], e["gpsum_guard"])
            t.wait_ge(S["rin"], e["rin_guard"])
            first = True
            for db in range(16):
                hiT = c.hiT[:, db * 512 + tb * 128: db * 512 + tb * 128 + 128]
                loT = c.loT[:, db * 512 + tb * 128: db * 512 + tb * 128 + 128]
                ghi = c.gwhiS[:, db * 8:(db + 1) * 8]
                glo = c.gwloS[:, db * 8:(db + 1) * 8]
                t.matmul(c.pb_misc[:, 0:8], hiT, ghi, start=first, stop=False)
                first = False
                t.matmul(c.pb_misc[:, 0:8], loT, ghi, start=False, stop=False)
                mm = t.matmul(c.pb_misc[:, 0:8], hiT, glo, start=False, stop=(db == 15))
            mm.then_inc(S["tmm3"], 1)
            t.wait_ge(S["gdone"], e["gdone"])
            t.wait_ge(S["mskd"], e["mskd"] - 1)
            t.transpose(c.pb_bf[0:8, 256:384], c.mskb8[:], c.identS[:]).then_inc(S["tmm3"], 1)
        elif k == "mskb":
            t.wait_ge(S["mskrl"], e["mskrl"])
            t.wait_ge(S["rin"], e["rin_guard"])
            t.wait_ge(S["aSig"], e["aSig_guard"])
            t.matmul(c.pb_misc[:, 0:512], c.onerS[:], c.mskrow[0:1, :],
                     start=True, stop=True).then_inc(S["tmm3"], 1)
        elif k == "ffnh":
            rin = c.hiT if e["e"] == 8 else c.rinT
            t.wait_ge(S["rin"], e["rin_need"])
            t.wait_ge(S[f"w13l{e['s1']}"], e["w1_need"])
            t.wait_ge(S[f"w13l{e['s3']}"], e["w3_need"])
            t.wait_ge(S["aSil"], e["aSil_guard"])
            t.wait_ge(S["hdone"], e["hdone_guard"])
            t.wait_ge(S["ydone"], e["ydone_guard"])
            for db in range(16):
                mm = t.matmul(c.pb_alw[:, 0:512], c.w13buf[e["s1"]][:, db * 128:(db + 1) * 128],
                              rin[:, db * 512:(db + 1) * 512], start=(db == 0), stop=(db == 15))
            mm.then_inc(S["tmm3"], 1)
            for db in range(16):
                mm = t.matmul(c.pb_uo[:, 0:512], c.w13buf[e["s3"]][:, db * 128:(db + 1) * 128],
                              rin[:, db * 512:(db + 1) * 512], start=(db == 0), stop=(db == 15))
            mm.then_inc(S["tmm3"], 1)
        elif k == "ffny":
            t.wait_ge(S["hdone"], e["hdone_need"])
            t.wait_ge(S[f"w2l{e['slot']}"], e["w2_need"])
            if e["si"] == 0:
                t.wait_ge(S["ydone"], e["y_guard"])
            hTe = c.hT[e["e"] % 4] if e["e"] < 8 else c.hT[0]
            for tb in range(NTB):
                for hb in range(8):
                    mm = t.matmul(psy[tb][:, 0:512],
                                  hTe[:, hb * 512 + tb * 128: hb * 512 + tb * 128 + 128],
                                  c.w2D[e["slot"]][:, hb * 512:(hb + 1) * 512],
                                  start=(e["si"] == 0 and hb == 0),
                                  stop=(e["si"] == e["nsub"] - 1 and hb == 7))
            mm.then_inc(S["tmm3"], 1)


def p3_scalar(c, sc, nc, evs, act_inc):
    S = c.sems
    sc.wait_ge(S["cc"], 4)
    for e in evs:
        k = e["kind"]
        if k == "ln2":
            sc.wait_ge(S["aWo"], e["aWo_need"])
            x1 = c.x_g[:, e["tb"] * 2048:(e["tb"] + 1) * 2048]
            sc.activation(out=c.sqj3[:], in_=x1, func=AF.Square, accum_out=c.stats[:, 1:2])
            act_inc(c, sc, S["sq3"])
            sc.wait_ge(S["dv3"], e["dv"])
            sc.activation(out=c.stats[:, 6:7], in_=c.stats[:, 3:4], func=AF.Ln, bias=1e-5)
            sc.drain()
            sc.activation(out=c.stats[:, 4:5], in_=c.stats[:, 6:7], func=AF.Exp, scale=-0.5)
            act_inc(c, sc, S["st3"])
        elif k == "gate":
            sc.wait_ge(S["tmm3"], e["tmm"])
            sc.wait_ge(S["gdone"], e["gdone"] - 1)
            sc.activation(out=c.sc8[0], in_=c.pb_misc[:, 0:8], func=AF.Sigmoid)
            act_inc(c, sc, S["aSig"])
        elif k == "ffnh":
            sc.wait_ge(S["tmm3"], e["tmm_h1"])
            sc.wait_ge(S["hdone"], e["hdone_guard"])
            sc.activation(out=c.hsil[:], in_=c.pb_alw[:, 0:512], func=AF.Silu)
            act_inc(c, sc, S["aSil"])


def p3_vector(c, v, nc, evs, dve_inc):
    S = c.sems
    v.wait_ge(S["cc"], 4)
    for e in evs:
        k = e["kind"]
        if k == "wo":
            v.wait_ge(S["tmm3"], e["tmm"])
            v.wait_ge(S["xld"], e["xld_need"])
            tb, dchunk = e["tb"], e["dchunk"]
            sl = c.x_g[:, tb * 2048 + dchunk * 512: tb * 2048 + (dchunk + 1) * 512]
            v.tensor_add(out=sl, in0=sl, in1=c.pb_proj[e["bank"]][:, 0:512])
            dve_inc(v, S["aWo"])
        elif k == "ln2":
            tb, par = e["tb"], e["par"]
            x1 = c.x_g[:, tb * 2048:(tb + 1) * 2048]
            v.wait_ge(S["aWo"], e["aWo_need"])
            v.reduce_sum(out=c.stats[:, 0:1], in_=x1, axis=AX.X)
            v.drain()
            v.tensor_scalar(out=c.stats[:, 2:3], in0=c.stats[:, 0:1], scalar1=1.0 / 2048,
                            scalar2=None, op0=ALU.mult)
            v.drain()
            v.tensor_mul(out=c.stats[:, 5:6], in0=c.stats[:, 2:3], in1=c.stats[:, 2:3])
            v.drain()
            v.wait_ge(S["sq3"], e["sq"])
            v.tensor_scalar(out=c.stats[:, 3:4], in0=c.stats[:, 1:2], scalar1=1.0 / 2048,
                            scalar2=c.stats[:, 5:6], op0=ALU.mult, op1=ALU.subtract)
            dve_inc(v, S["dv3"])
            v.wait_ge(S["st3"], e["st1"])
            if e["hist_guard"]:
                v.wait_ge(S[f"histn{par}"], e["hist_guard"])
            v.tensor_scalar(out=c.xf3[:], in0=x1, scalar1=c.stats[:, 2:3],
                            scalar2=c.stats[:, 4:5], op0=ALU.subtract, op1=ALU.mult)
            v.drain()
            v.tensor_copy(out=c.hi_stg[par][:], in_=c.xf3[:])
            v.drain()
            v.tensor_sub(out=c.lo_stg[par][:], in0=c.xf3[:], in1=c.hi_stg[par][:])
            dve_inc(v, S["aLn"])
        elif k == "gate":
            g, tb = e["g"], e["tb"]
            v.wait_ge(S["aSig"], e["aSig"])
            scores, vv, eq, cs, cs2, m1 = c.sc8
            m2 = c.m2t
            v.tensor_add(out=vv, in0=scores, in1=c.biasbS[:])
            v.drain()

            def argmax_mask(vin, mout):
                v.reduce_max(out=c.r1t[:, 0:1], in_=vin, axis=AX.X)
                v.drain()
                v.tensor_scalar(out=eq, in0=vin, scalar1=c.r1t[:, 0:1], scalar2=None,
                                op0=ALU.is_ge)
                v.drain()
                v.tensor_copy(out=cs, in_=eq)
                v.drain()
                v.tensor_copy(out=cs2[:, 0:1], in_=cs[:, 0:1])
                v.tensor_add(out=cs2[:, 1:8], in0=cs[:, 1:8], in1=cs[:, 0:7])
                v.drain()
                v.tensor_copy(out=cs[:, 0:2], in_=cs2[:, 0:2])
                v.tensor_add(out=cs[:, 2:8], in0=cs2[:, 2:8], in1=cs2[:, 0:6])
                v.drain()
                v.tensor_copy(out=cs2[:, 0:4], in_=cs[:, 0:4])
                v.tensor_add(out=cs2[:, 4:8], in0=cs[:, 4:8], in1=cs[:, 0:4])
                v.drain()
                v.tensor_scalar(out=cs, in0=cs2, scalar1=1.5, scalar2=None, op0=ALU.is_le)
                v.drain()
                v.tensor_mul(out=mout, in0=eq, in1=cs)
                v.drain()
            argmax_mask(vv, m1)
            v.scalar_tensor_tensor(out=vv, in0=m1, scalar=-1e9, in1=vv,
                                   op0=ALU.mult, op1=ALU.add)
            v.drain()
            argmax_mask(vv, m2)
            v.tensor_add(out=m1, in0=m1, in1=m2)
            v.drain()
            v.tensor_mul(out=m1, in0=m1, in1=scores)
            v.drain()
            v.tensor_copy(out=c.mskb8[:], in_=m1)
            dve_inc(v, S["gdone"])
            v.wait_ge(S["tmm3"], e["tmsk"])
            v.wait_ge(S["mskst"], e["mskp_guard"])
            v.tensor_copy(out=c.mskT8[0:8, :], in_=c.pb_bf[0:8, 256:384])
            dve_inc(v, S["mskd"])
        elif k == "mskb":
            v.wait_ge(S["tmm3"], e["tmm"])
            v.tensor_copy(out=c.mskbc[:], in_=c.pb_misc[:, 0:512])
            v.drain()
            for db in range(16):
                v.tensor_mul(out=c.rinT[:, db * 512:(db + 1) * 512],
                             in0=c.hiT[:, db * 512:(db + 1) * 512], in1=c.mskbc[:])
                v.drain()
            dve_inc(v, S["rin"])
        elif k == "ffnh":
            v.wait_ge(S["aSil"], e["aSil"])
            v.wait_ge(S["tmm3"], e["tmm_h3"])
            hTe = c.hT[e["e"] % 4] if e["e"] < 8 else c.hT[0]
            hb = e["hb"]
            v.tensor_mul(out=hTe[:, hb * 512:(hb + 1) * 512], in0=c.hsil[:],
                         in1=c.pb_uo[:, 0:512])
            dve_inc(v, S["hdone"])
        elif k == "ffny":
            if e["ydone"] is None:
                continue
            v.wait_ge(S["tmm3"], e["tmm"])
            psy = [c.pb_proj[0], c.pb_proj[1], c.pb_uo, c.pb_ds]
            dchunk = e["dchunk"]
            for tb in range(NTB):
                dst = c.y_acc[:, tb * 2048 + dchunk * 512: tb * 2048 + (dchunk + 1) * 512]
                if e["first_round"]:
                    v.tensor_copy(out=dst, in_=psy[tb][:, 0:512])
                else:
                    v.tensor_add(out=dst, in0=dst, in1=psy[tb][:, 0:512])
                v.drain()
            dve_inc(v, S["ydone"])
        elif k == "fout":
            buf = e["buf"]
            v.wait_ge(S[f"x1l{buf}"], e["x1_need"])
            v.wait_ge(S["ydone"], e["ydone_need"])
            v.tensor_add(out=c.x1b[buf][:], in0=c.x1b[buf][:],
                         in1=c.y_acc[:, e["tb"] * 2048:(e["tb"] + 1) * 2048])
            dve_inc(v, S["odone"])


def p3_gpsimd(c, g_, nc, evs):
    S = c.sems
    g_.wait_ge(S["cc"], 4)
    for e in evs:
        k = e["kind"]
        if k == "ln2":
            g, tb, par = e["g"], e["tb"], e["par"]
            g_.wait_ge(S["aLn"], e["aLn"])
            t0 = g * GRP + tb * 128
            g_.dma_start(out=c.xfhi_scr[t0:t0 + 128, :], in_=c.hi_stg[par][:]).then_inc(S[f"histn{par}"], 16)
            g_.dma_start(out=c.xflo_scr[t0:t0 + 128, :], in_=c.lo_stg[par][:]).then_inc(S[f"histn{par}"], 16)
            g_.dma_start(out=c.x1_scr[t0:t0 + 128, :],
                         in_=c.x_g[:, tb * 2048:(tb + 1) * 2048]).then_inc(S[f"histn{par}"], 16)
        elif k == "gate":
            g, tb = e["g"], e["tb"]
            g_.wait_ge(S["mskd"], e["mskd"])
            g_.dma_start(out=c.msk_scr[0:8, g * GRP + tb * 128:g * GRP + (tb + 1) * 128],
                         in_=c.mskT8[0:8, :]).then_inc(S["mskst"], 16)
        elif k == "fout":
            g, tb, buf = e["g"], e["tb"], e["buf"]
            g_.wait_ge(S["odone"], e["odone"])
            t0 = g * GRP + tb * 128
            g_.dma_start(out=c.out[t0:t0 + 128, :], in_=c.x1b[buf][:]).then_inc(S[f"outst{buf}"], 16)
        elif k == "p3fin":
            g_.wait_ge(S["outst0"], e["out_totals"][0])
            g_.wait_ge(S["outst1"], e["out_totals"][1])
        elif k == "prefix_fin":
            le = e["ev"]
            lk = le["kind"]
            if lk == "ln2":
                g_.wait_ge(S["histn0"], 16 * 48)
                g_.wait_ge(S["histn1"], 16 * 48)
            elif lk == "xfT":
                g_.wait_ge(S["xfl"], le["xfl"])
            elif lk == "gate":
                g_.wait_ge(S["mskst"], le["mskst"])
            elif lk == "mskb":
                g_.wait_ge(S["rin"], le["rin"])
            elif lk == "ffnh":
                g_.wait_ge(S["hdone"], le["hdone"])
            elif lk == "ffny":
                g_.wait_ge(S["tmm3"], le["tmm"])
            elif lk == "fout":
                g_.wait_ge(S[f"outst{le['buf']}"], le["out_total"])

class Ctx:
    pass


def build(cfg):
    nc = bass.Bass("TRN2", target_bir_lowering=False, debug=False)
    c = Ctx()
    c.cfg = cfg
    dram = nc.dram_tensor
    c.x = dram("x", [TW, D], F32, kind="ExternalInput").ap()
    c.wq = dram("wq", [D, 256], F32, kind="ExternalInput").ap()
    c.wk = dram("wk", [D, 256], F32, kind="ExternalInput").ap()
    c.wv = dram("wv", [D, 256], F32, kind="ExternalInput").ap()
    c.wb = dram("wb", [D, 2], F32, kind="ExternalInput").ap()
    c.convw = dram("convw", [128, 48], F32, kind="ExternalInput").ap()
    c.normw = dram("normw", [128, 256], F32, kind="ExternalInput").ap()
    c.wo = dram("wo", [D, D], F32, kind="ExternalInput").ap()
    c.gwT = dram("gwT", [D, 8], F32, kind="ExternalInput").ap()
    c.biasb = dram("biasb", [128, 8], F32, kind="ExternalInput").ap()
    c.fw13 = dram("fw13", [W13_ROWS // NC, D], F32, kind="ExternalInput").ap()
    c.fw2 = dram("fw2", [W2_ROWS // NC, D], F32, kind="ExternalInput").ap()
    c.ident = dram("ident", [128, 128], BF16, kind="ExternalInput").ap()
    c.negstril = dram("negstril", [128, 128], BF16, kind="ExternalInput").ap()
    c.triu = dram("triu", [128, 128], BF16, kind="ExternalInput").ap()
    c.ones_col = dram("ones_col", [128, 1], BF16, kind="ExternalInput").ap()
    c.ones_row = dram("ones_row", [1, 128], BF16, kind="ExternalInput").ap()
    c.coreid = dram("coreid", [1, 1], F32, kind="ExternalInput").ap()
    c.fconst = dram("fconst", [128, 8], F32, kind="ExternalInput").ap()
    c.out = dram("out", [TW, D], F32, kind="ExternalOutput").ap()
    c.ag1_in = dram("ag1_in", [TW, D], BF16)
    c.ag1_out = dram("ag1_out", [NC * TW, D], BF16)
    c.ag2_in = dram("ag2_in", [NC * TW, 256], BF16)
    c.ag2_out = dram("ag2_out", [NC * TW, 256], BF16)
    c.agw13_in = dram("agw13_in", [W13_ROWS // NC, D], BF16)
    c.agw13_out = dram("agw13_out", [W13_ROWS, D], BF16)
    c.agw2_in = dram("agw2_in", [W2_ROWS // NC, D], BF16)
    c.agw2_out = dram("agw2_out", [W2_ROWS, D], BF16)
    c.kv_scr = dram("kv_scr", [1024, T], BF16)
    c.x1_scr = dram("x1_scr", [TW, D], F32)
    c.xfhi_scr = dram("xfhi_scr", [TW, D], BF16)
    c.xflo_scr = dram("xflo_scr", [TW, D], BF16)
    c.wo_scr = dram("wo_scr", [D, D], BF16)
    c.msk_scr = dram("msk_scr", [8, TW], BF16)

    A = nc.alloc_sbuf_tensor
    c.ar_bf = A("ar_bf", [128, 69744], BF16)
    c.ar_f32 = A("ar_f32", [128, 14336], F32)
    c.xa = [c.ar_f32[:, i * 2048:(i + 1) * 2048] for i in range(2)]
    c.stats = A("stats", [128, 8], F32)
    c.xn_blk = [c.ar_bf[:, 4096 + i * 2048: 4096 + (i + 1) * 2048] for i in range(2)]
    c.identS = A("identS", [128, 128], BF16)
    c.negstrilS = A("negstrilS", [128, 128], BF16)
    c.triuS = A("triuS", [128, 128], BF16)
    c.onecS = A("onecS", [128, 1], BF16)
    c.onerS = A("onerS", [1, 128], BF16)
    c.normwS = A("normwS", [128, 256], F32)
    c.biasbS = A("biasbS", [128, 8], F32)
    c.gwhiS = A("gwhiS", [128, 16 * 8], BF16)
    c.gwloS = A("gwloS", [128, 16 * 8], BF16)
    c.wstg = [c.ar_f32[:, 4096 + i * 2048: 4096 + (i + 1) * 2048] for i in range(2)]
    c.wstgb = [c.ar_bf[:, i * 2048:(i + 1) * 2048] for i in range(2)]
    c.wqS = c.ar_bf[:, 45168:49264]
    c.wkS = c.ar_bf[:, 49264:53360]
    c.wvS = c.ar_bf[:, 53360:57456]
    c.wbS = A("wbS", [128, 16 * 2], BF16)
    c.convS = A("convS", [128, 48], F32)

    c.fconstS = A("fconstS", [128, 8], F32)
    nc.const_aps.register(F32, 1e-5, c.fconstS[:, 0:1])
    nc.const_aps.register(F32, 1e-6, c.fconstS[:, 1:2])
    nc.const_aps.register(F32, -0.5, c.fconstS[:, 2:3])
    nc.const_aps.register(F32, 0.5, c.fconstS[:, 3:4])
    nc.const_aps.register(F32, -2.772588722239781, c.fconstS[:, 4:5])
    c.sems = {}
    return nc, c


def sem(nc, c, name):
    h = nc.alloc_semaphore(name)
    c.sems[name] = h
    return h


def dve_inc(v, h, n=1):
    v.drain()
    v.engine_nop().then_inc(h, n)


def act_inc(c, sc, h, n=1):
    sc.drain()
    sc.activation(out=c.stats[:, 7:8], in_=c.fconstS[:, 0:1], func=AF.Copy).then_inc(h, n)


P0_W13_TILES = W13_ROWS // NC // 128
P0_W2_TILES = W2_ROWS // NC // 128


def p0_sync(c, s, nc):
    lda = [c.sems["p0_ld0"], c.sems["p0_ld1"]]
    cst = c.sems["p0_consts"]
    s.dma_start(out=c.identS[:], in_=c.ident).then_inc(cst, 16)
    s.dma_start(out=c.negstrilS[:], in_=c.negstril).then_inc(cst, 16)
    s.dma_start(out=c.triuS[:], in_=c.triu).then_inc(cst, 16)
    s.dma_start(out=c.onecS[:], in_=c.ones_col).then_inc(cst, 16)
    s.dma_start(out=c.onerS[:], in_=c.ones_row).then_inc(cst, 16)
    s.dma_start(out=c.normwS[:], in_=c.normw).then_inc(cst, 16)
    s.dma_start(out=c.biasbS[:], in_=c.biasb).then_inc(cst, 16)
    s.dma_start(out=c.fconstS[:], in_=c.fconst).then_inc(cst, 16)
    done = c.sems["p0_cast"]
    n = 0
    for name, src, cols, ntile in [("wq", c.wq, 256, 16), ("wk", c.wk, 256, 16),
                                   ("wv", c.wv, 256, 16), ("wb", c.wb, 2, 16),
                                   ("gw", c.gwT, 8, 16)]:
        for i in range(ntile):
            if n >= 2:
                s.wait_ge(done, n - 1)
            s.dma_start(out=c.wstg[n % 2][:, 0:cols], in_=src[i * 128:(i + 1) * 128, :]).then_inc(lda[n % 2], 16)
            n += 1
    s.dma_start(out=c.convS[:], in_=c.convw).then_inc(cst, 16)
    for i in range(P0_W13_TILES):
        if n >= 2:
            s.wait_ge(done, n - 1)
        s.dma_start(out=c.wstg[n % 2][:, 0:D], in_=c.fw13[i * 128:(i + 1) * 128, :]).then_inc(lda[n % 2], 16)
        n += 1
    for i in range(P0_W2_TILES):
        if n >= 2:
            s.wait_ge(done, n - 1)
        s.dma_start(out=c.wstg[n % 2][:, 0:D], in_=c.fw2[i * 128:(i + 1) * 128, :]).then_inc(lda[n % 2], 16)
        n += 1
    for i in range(16):
        if n >= 2:
            s.wait_ge(done, n - 1)
        s.dma_start(out=c.wstg[n % 2][:, 0:D], in_=c.wo[i * 128:(i + 1) * 128, :]).then_inc(lda[n % 2], 16)
        n += 1
    c.p0_n_wtiles = n


def p0_vector(c, v, nc):
    lda = [c.sems["p0_ld0"], c.sems["p0_ld1"]]
    done = c.sems["p0_cast"]
    n = 0
    specs = ([("wq", 256)] * 16 + [("wk", 256)] * 16 + [("wv", 256)] * 16 +
             [("wb", 2)] * 16 + [("gw", 8)] * 16)
    idx = {"wq": 0, "wk": 0, "wv": 0, "wb": 0, "wo": 0, "gw": 0}
    for name, cols in specs:
        v.wait_ge(lda[n % 2], 16 * (n // 2 + 1))
        w = c.wstg[n % 2]
        i = idx[name]
        if name == "wq":
            v.tensor_copy(out=c.wqS[:, i * 256:(i + 1) * 256], in_=w[:, 0:256])
        elif name == "wk":
            v.tensor_copy(out=c.wkS[:, i * 256:(i + 1) * 256], in_=w[:, 0:256])
        elif name == "wv":
            v.tensor_copy(out=c.wvS[:, i * 256:(i + 1) * 256], in_=w[:, 0:256])
        elif name == "wb":
            v.tensor_copy(out=c.wbS[:, i * 2:(i + 1) * 2], in_=w[:, 0:2])
        elif name == "gw":
            v.tensor_copy(out=c.gwhiS[:, i * 8:(i + 1) * 8], in_=w[:, 0:8])
            v.drain()
            v.tensor_sub(out=c.gwloS[:, i * 8:(i + 1) * 8], in0=w[:, 0:8],
                         in1=c.gwhiS[:, i * 8:(i + 1) * 8])
        idx[name] += 1
        v.drain()
        v.engine_nop().then_inc(done, 1)
        n += 1
    for i in range(P0_W13_TILES + P0_W2_TILES + 16):
        v.wait_ge(lda[n % 2], 16 * (n // 2 + 1))
        if i >= 2:
            v.wait_ge(c.sems[f"p0_st{i % 2}"], 16 * (i // 2))
        v.tensor_copy(out=c.wstgb[i % 2][:, 0:D], in_=c.wstg[n % 2][:, 0:D])
        v.drain()
        v.engine_nop().then_inc(done, 1)
        n += 1
    c.p0_cast_total = n


def p0_gpsimd(c, g, nc):
    done = c.sems["p0_cast"]
    st = [c.sems["p0_st0"], c.sems["p0_st1"]]
    ncast_w = 80
    for i in range(P0_W13_TILES + P0_W2_TILES + 16):
        g.wait_ge(done, ncast_w + i + 1)
        if i < P0_W13_TILES:
            g.dma_start(out=c.agw13_in[i * 128:(i + 1) * 128, :], in_=c.wstgb[i % 2][:, 0:D]).then_inc(st[i % 2], 16)
        elif i < P0_W13_TILES + P0_W2_TILES:
            j = i - P0_W13_TILES
            g.dma_start(out=c.agw2_in[j * 128:(j + 1) * 128, :], in_=c.wstgb[i % 2][:, 0:D]).then_inc(st[i % 2], 16)
        else:
            j = i - P0_W13_TILES - P0_W2_TILES
            g.dma_start(out=c.wo_scr[j * 128:(j + 1) * 128, :], in_=c.wstgb[i % 2][:, 0:D]).then_inc(st[i % 2], 16)
    tot = P0_W13_TILES + P0_W2_TILES + 16
    g.wait_ge(st[0], 16 * ((tot + 1) // 2))
    g.wait_ge(st[1], 16 * (tot // 2))
    cc = c.sems["cc"]
    g.collective_compute("AllGather", ALU.bypass, replica_groups=[list(range(NC))],
                         ins=[c.agw13_in.ap()], outs=[c.agw13_out.ap()]).then_inc(cc)
    g.collective_compute("AllGather", ALU.bypass, replica_groups=[list(range(NC))],
                         ins=[c.agw2_in.ap()], outs=[c.agw2_out.ap()]).then_inc(cc)


def p1_sync(c, s, nc):
    ld = [c.sems["p1_ld0"], c.sems["p1_ld1"]]
    fb = c.sems["p1_free"]
    for i in range(NBLK):
        if i >= 2:
            s.wait_ge(fb, i - 1)
        s.dma_start(out=c.xa[i % 2][:], in_=c.x[i * 128:(i + 1) * 128, :]).then_inc(ld[i % 2], 16)


def p1_vector(c, v, nc):
    ld = [c.sems["p1_ld0"], c.sems["p1_ld1"]]
    st1 = c.sems["p1_stats1"]
    dv = c.sems["p1_stats2"]
    fb = c.sems["p1_free"]
    xn = c.sems["p1_xn"]
    for i in range(NBLK):
        v.wait_ge(ld[i % 2], 16 * (i // 2 + 1))
        v.reduce_sum(out=c.stats[:, 0:1], in_=c.xa[i % 2][:], axis=AX.X)
        v.drain()
        v.tensor_scalar(out=c.stats[:, 2:3], in0=c.stats[:, 0:1], scalar1=1.0 / D,
                        scalar2=None, op0=ALU.mult)
        v.drain()
        v.tensor_mul(out=c.stats[:, 5:6], in0=c.stats[:, 2:3], in1=c.stats[:, 2:3])
        v.drain()
        v.wait_ge(c.sems["p1_sq"], i + 1)
        v.tensor_scalar(out=c.stats[:, 3:4], in0=c.stats[:, 1:2], scalar1=1.0 / D,
                        scalar2=c.stats[:, 5:6], op0=ALU.mult, op1=ALU.subtract)
        dve_inc(v, dv)
        v.wait_ge(st1, i + 1)
        if i >= 2:
            v.wait_ge(c.sems[f"p1_st{i % 2}"], 16 * (i // 2))
        v.tensor_scalar(out=c.xn_blk[i % 2][:], in0=c.xa[i % 2][:], scalar1=c.stats[:, 2:3],
                        scalar2=c.stats[:, 4:5], op0=ALU.subtract, op1=ALU.mult)
        v.drain()
        v.engine_nop().then_inc(xn, 1)
        v.engine_nop().then_inc(fb, 1)


def p1_scalar_fix(c, sc, nc):
    ld = [c.sems["p1_ld0"], c.sems["p1_ld1"]]
    sq = c.sems["p1_sq"]
    st1 = c.sems["p1_stats1"]
    dv = c.sems["p1_stats2"]
    sc.wait_ge(c.sems["p0_consts"], 16 * 8)
    for i in range(NBLK):
        sc.wait_ge(ld[i % 2], 16 * (i // 2 + 1))
        sc.activation(out=c.ar_bf[:, 8192:8192 + D], in_=c.xa[i % 2][:], func=AF.Square,
                      accum_out=c.stats[:, 1:2])
        act_inc(c, sc, sq)
        sc.wait_ge(dv, i + 1)
        sc.activation(out=c.stats[:, 6:7], in_=c.stats[:, 3:4], func=AF.Ln, bias=1e-5)
        sc.drain()
        sc.activation(out=c.stats[:, 4:5], in_=c.stats[:, 6:7], func=AF.Exp, scale=-0.5)
        act_inc(c, sc, st1)


def p1_gpsimd(c, g, nc):
    xn = c.sems["p1_xn"]
    st = [c.sems["p1_st0"], c.sems["p1_st1"]]
    cc = c.sems["cc"]
    for i in range(NBLK):
        g.wait_ge(xn, i + 1)
        g.dma_start(out=c.ag1_in[i * 128:(i + 1) * 128, :], in_=c.xn_blk[i % 2][:]).then_inc(st[i % 2], 16)
    g.wait_ge(st[0], 16 * (NBLK // 2))
    g.wait_ge(st[1], 16 * (NBLK // 2))
    g.collective_compute("AllGather", ALU.bypass,
                         replica_groups=[list(range(NC))],
                         ins=[c.ag1_in.ap()], outs=[c.ag1_out.ap()]).then_inc(cc)


def host_constants():
    consts = {
        "fconst": np.tile(np.array([[1e-5, 1e-6, -0.5, 0.5, -2.772588722239781, 0, 0, 0]], np.float32), (128, 1)),
        "ident": np.eye(128, dtype=BF),
        "negstril": (-np.tril(np.ones((128, 128), np.float32), -1)).astype(BF),
        "triu": np.triu(np.ones((128, 128), np.float32)).astype(BF),
        "ones_col": np.ones((128, 1), BF),
        "ones_row": np.ones((1, 128), BF),
        "coreid": np.zeros((1, 1), np.float32),
    }
    return consts


def host_in_maps(inputs):
    x = np.ascontiguousarray(inputs["L0_attn_hidden_states"].reshape(B * T, D))
    Wq, Wk, Wv = inputs["L0_attn_Wq"], inputs["L0_attn_Wk"], inputs["L0_attn_Wv"]
    Wb, Wo = inputs["L0_attn_Wb"], inputs["L0_attn_Wo"]
    cq, ck, cv = inputs["L0_attn_conv_q"], inputs["L0_attn_conv_k"], inputs["L0_attn_conv_v"]
    normw = inputs["L0_attn_norm_w"]
    gw = inputs["L0_ffn_gate_w"]
    ebias = inputs["L0_ffn_expert_bias"]
    w1, w2, w3 = inputs["L0_ffn_w1"], inputs["L0_ffn_w2"], inputs["L0_ffn_w3"]
    sw1, sw2, sw3 = inputs["L0_ffn_sw1"], inputs["L0_ffn_sw2"], inputs["L0_ffn_sw3"]
    def hbmajor(MT):
        A4 = MT.reshape(16, 128, 8, 128)
        return np.ascontiguousarray(A4.transpose(2, 1, 0, 3).reshape(8 * 128, 16 * 128))
    blocks13 = []
    for e in range(E):
        blocks13.append(hbmajor(np.ascontiguousarray(w1[e].T)))
        blocks13.append(hbmajor(np.ascontiguousarray(w3[e].T)))
    blocks13.append(hbmajor(np.ascontiguousarray(sw1.T)))
    blocks13.append(hbmajor(np.ascontiguousarray(sw3.T)))
    img13 = np.concatenate(blocks13, axis=0)
    blocks2 = [np.ascontiguousarray(w2[e].T) for e in range(E)] + [np.ascontiguousarray(sw2.T)]
    img2 = np.concatenate(blocks2, axis=0)
    consts = host_constants()
    in_maps = []
    for cid in range(NC):
        b = cid // 4
        t0 = (cid % 4) * TW
        hc = cid
        colsl = slice(hc * 256, hc * 256 + 256)
        convw = np.zeros((128, 48), np.float32)
        for pi, cw in enumerate([cq, ck, cv]):
            sl = cw[colsl, :]
            for blk in range(4):
                mb = blk % 2
                convw[:, pi * 16 + blk * 4:pi * 16 + blk * 4 + 4] = sl[mb * 128:(mb + 1) * 128, :]
        wb2 = np.concatenate([Wb[:, hc:hc + 1], Wb[:, hc:hc + 1]], axis=1)
        m = {
            "x": np.ascontiguousarray(x[b * T + t0: b * T + t0 + TW]),
            "wq": np.ascontiguousarray(Wq[:, colsl]),
            "wk": np.ascontiguousarray(Wk[:, colsl]),
            "wv": np.ascontiguousarray(Wv[:, colsl]),
            "wb": np.ascontiguousarray(wb2),
            "convw": convw,
            "normw": np.ascontiguousarray(np.tile(np.asarray(normw)[None, :], (128, 1))),
            "wo": np.ascontiguousarray(Wo),
            "gwT": np.ascontiguousarray(gw.T),
            "biasb": np.ascontiguousarray(np.tile(np.asarray(ebias)[None, :], (128, 1))),
            "fw13": np.ascontiguousarray(img13[cid * (W13_ROWS // NC):(cid + 1) * (W13_ROWS // NC)]),
            "fw2": np.ascontiguousarray(img2[cid * (W2_ROWS // NC):(cid + 1) * (W2_ROWS // NC)]),
        }
        m.update(consts)
        in_maps.append(m)
    return in_maps


def build_full(p3_mode="full"):
    cfg = {}
    nc, c = build(cfg)
    p2_alloc(nc, c)
    p3_alloc(nc, c)
    ev2, _ = build_p2_events()
    ev3, _ = build_p3_events()
    dbgA = None
    if p3_mode != "full":
        stop_kind, nth = {"passA": ("ln2", 32), "xft": ("xfT", 1),
                          "gate": ("gate", 4), "gate1": ("gate", 1), "mskb": ("mskb", 4),
                          "ffnh": ("ffnh", 8), "ffny": ("ffny", 4),
                          "fout": ("fout", 4)}[p3_mode]
        cnt_k = 0
        idx = None
        for i, e in enumerate(ev3):
            if e["kind"] == stop_kind:
                cnt_k += 1
                if cnt_k == nth:
                    idx = i
                    break
        assert idx is not None
        last = ev3[idx]
        fin = {"kind": "prefix_fin", "ev": last}
        ev3 = ev3[:idx + 1] + [fin]
        if p3_mode == "passA":
            dbgA = (nc.dram_tensor("dbg_x1", [TW, D], F32, kind="ExternalOutput").ap(),
                    nc.dram_tensor("dbg_hi", [TW, D], BF16, kind="ExternalOutput").ap(),
                    nc.dram_tensor("dbg_wo", [D, D], BF16, kind="ExternalOutput").ap(),
                    nc.dram_tensor("dbg_ag2", [2048, 512], BF16, kind="ExternalOutput").ap())
            c.dbgA = dbgA
        if p3_mode == "gate":
            dbgA = ("gate", nc.dram_tensor("dbg_msk", [8, TW], BF16, kind="ExternalOutput").ap())
            c.dbgA = dbgA
    names = (["p0_ld0", "p0_ld1", "p0_consts", "p0_cast", "p0_st0", "p0_st1",
              "cc", "p1_ld0", "p1_ld1", "p1_free", "p1_stats1", "p1_stats2", "p1_sq",
              "p1_xn", "p1_st0", "p1_st1"] + P2_SEM_NAMES + P3_SEM_NAMES)
    groups8 = [list(range(NC))]
    with nc.Block() as block:
        for name in names:
            sem(nc, c, name)

        @block.sync
        def _(s_):
            p0_sync(c, s_, nc)
            p1_sync(c, s_, nc)
            s_.wait_ge(c.sems["cc"], 3)
            p2_sync(c, s_, nc, ev2)
            p3_sync(c, s_, nc, ev3)

        @block.vector
        def _(v):
            p0_vector(c, v, nc)
            p1_vector(c, v, nc)
            p2_vector(c, v, nc, ev2, dve_inc)
            p3_vector(c, v, nc, ev3, dve_inc)

        @block.scalar
        def _(sc):
            p1_scalar_fix(c, sc, nc)
            p2_scalar(c, sc, nc, ev2, act_inc)
            p3_scalar(c, sc, nc, ev3, act_inc)

        @block.tensor
        def _(t):
            p2_tensor(c, t, nc, ev2)
            p3_tensor(c, t, nc, ev3)

        @block.gpsimd
        def _(g):
            p0_gpsimd(c, g, nc)
            p1_gpsimd(c, g, nc)
            p2_gpsimd(c, g, nc, ev2, groups8)
            p3_gpsimd(c, g, nc, ev3)
            if dbgA is not None and dbgA[0] == "gate":
                fin = c.sems["odone"]
                g.dma_start(out=dbgA[1], in_=c.msk_scr.ap()).then_inc(fin, 16)
                g.wait_ge(fin, 16)
            elif dbgA is not None:
                fin = c.sems["odone"]
                g.dma_start(out=dbgA[0], in_=c.x1_scr.ap()).then_inc(fin, 16)
                g.dma_start(out=dbgA[1], in_=c.xfhi_scr.ap()).then_inc(fin, 16)
                g.dma_start(out=dbgA[2], in_=c.wo_scr.ap()).then_inc(fin, 16)
                for r in range(4):
                    g.dma_start(out=dbgA[3][r * 512:(r + 1) * 512, :],
                                in_=c.ag2_out[r * 4096:r * 4096 + 512, :]).then_inc(fin, 16)
                g.wait_ge(fin, 112)
    return nc


def kernel(**inputs):
    from concourse.bass_utils import run_bass_kernel_spmd
    in_maps = host_in_maps(inputs)
    nc = build_full()
    res = run_bass_kernel_spmd(nc, in_maps, core_ids=list(range(NC)))
    out = np.zeros((B * T, D), np.float32)
    for cid in range(NC):
        b = cid // 4
        t0 = (cid % 4) * TW
        out[b * T + t0: b * T + t0 + TW] = res.results[cid]["out"]
    return out.reshape(B, T, D)


# version 05
# speedup vs baseline: 1.0274x; 1.0274x over previous
import os
os.environ["TRN_ALLOW_ICI_ALL_TO_ALL"] = "1"
import numpy as np
import ml_dtypes
import concourse.bass as bass
import concourse.mybir as mybir

F32 = mybir.dt.float32
BF16 = mybir.dt.bfloat16
AF = mybir.ActivationFunctionType
ALU = mybir.AluOpType
AX = mybir.AxisListType
BF = ml_dtypes.bfloat16

B, T, D = 2, 16384, 2048
H, DK, DV = 8, 256, 256
E, HID = 8, 1024
NC = 8
TW = 4096
NBLK = TW // 128
CH = 128
W13_ROWS = (2 * E + 2) * 8 * 128
W2_ROWS = (E + 1) * HID




TT = 1024
NTT = 16
NCH = 8
CH = 128
LN_DK = -2.772588722239781
NLEV = 6


def xi(proj, mb):
    return 2 * proj + mb


def build_p2_events():
    ev = []
    cnt = {}

    def bump(k, n=1):
        cnt[k] = cnt.get(k, 0) + n
        return cnt[k]

    def val(k):
        return cnt.get(k, 0)

    def E(kind, **kw):
        kw["kind"] = kind
        ev.append(kw)
        return kw

    guard = {"proj0": ("pevac", 0), "proj1": ("pevac", 0), "misc": ("l2r", 0),
             "bf_z0": ("dZ0", 0), "bf_bt": ("bevac", 0)}

    last_proj_tmm = 0
    for tt in range(NTT):
        for p in range(2):
            E("xnT", tt=tt, p=p, wait_tmm=(last_proj_tmm if (tt, p) != (0, 0) else None),
              xnt_total=bump("xnt", 16) * 16)
            e = E("beta", tt=tt, p=p, xnt_need=val("xnt") * 16,
                  misc_guard=guard["misc"], bt_guard=guard["bf_bt"],
                  bmm=[bump("tmm") for _ in range(2)],
                  bsig=[bump("bsig") for _ in range(2)],
                  bcast=bump("bcast"),
                  btr=[bump("tmm") for _ in range(NCH)],
                  bevac=bump("bevac"))
            guard["misc"] = ("bsig", e["bsig"][1])
            guard["bf_bt"] = ("bevac", e["bevac"])
            E("stash", tt=tt, p=p, estash=bump("estash"),
              cdone_need=val("cdone"), pevac_need=val("pevac"))
            for proj in range(3):
                for mb in range(2):
                    for ch in range(2):
                        bank = ch
                        e = E("proj", tt=tt, p=p, proj=proj, mb=mb, ch=ch, bank=bank,
                              xnt_need=val("xnt") * 16,
                              estash_need=val("estash"),
                              esave_need=val("esave"),
                              cdone_need=val("cdone"),
                              bank_guard=guard[f"proj{bank}"],
                              pmm=bump("tmm"),
                              pevac=bump("pevac"))
                        guard[f"proj{bank}"] = ("pevac", e["pevac"])
                        last_proj_tmm = e["pmm"]
            E("stash_save", tt=tt, p=p, pevac_need=val("pevac"), esave=bump("esave"))
            for proj in range(3):
                for mb in range(2):
                    E("conv", tt=tt, p=p, proj=proj, mb=mb,
                      pevac_need=val("pevac"), esave_need=val("esave"),
                      kvst_guard=val("kvst") * 16,
                      cdone=bump("cdone"), csil=bump("csil"))
            for which in range(2):
                lch = []
                e = {"kind": "l2", "tt": tt, "p": p, "which": which,
                     "csil_need": val("csil"), "kvst_guard": val("kvst") * 16,
                     "l2sq": bump("l2sq")}
                for ch in range(2):
                    d = {"misc_guard": guard["misc"],
                         "bank_guard": guard[f"proj{ch}"],
                         "l2mm": bump("tmm"),
                         "l2r": bump("l2r"),
                         "l2rb": bump("l2rb"),
                         "l2bc": bump("tmm")}
                    guard["misc"] = ("l2r", d["l2r"])
                    lch.append(d)
                e["lch"] = lch
                e["l2d"] = bump("l2d")
                for ch in range(2):
                    guard[f"proj{ch}"] = ("l2d", e["l2d"])
                ev.append(e)
            E("kvstore", tt=tt, p=p, l2d_need=val("l2d"), csil_need=val("csil"),
              kvst_total=bump("kvst", 4) * 16)
        for p in range(2):
            E("ktv", tt=tt, p=p, kvst_need=val("kvst") * 16,
              tmm_need=val("tmm"), dXL_need=(val("dXL0"), val("dXL1")),
              ktvl_total=bump("ktvl", 8) * 16)
            if tt == 0:
                E("sinit", p=p, sS=bump(f"sS{p}"))
        for i in range(NCH):
            for p in range(2):
                o = tt * NCH + i
                e = {"kind": "chunk", "tt": tt, "p": p, "i": i, "o": o,
                     "ktvl_need": val("ktvl") * 16,
                     "bevac_need": val("bevac"), "l2d_need": val("l2d"),
                     "dXL_prev": val(f"dXL{p}"), "z0_prev": val(f"dZ0{p}"),
                     "aXZ_prev": val(f"aXZ{p}"), "dPT_prev": val(f"dPT{p}"),
                     "dW_prev": val(f"dW{p}"), "dU_prev": val(f"dU{p}"),
                     "aO_prev": val(f"aO{p}"), "sS_need": val(f"sS{p}"),
                     "ds_guard": (val("sS0"), val("sS1")),
                     "misc_guard": guard["misc"],
                     "ost_prev": val("ost") * 16}
                guard["misc"] = (f"sS{p}", val(f"sS{p}") + 1)
                e["tmA"] = bump("tmm")
                e["dXL"] = bump(f"dXL{p}")
                e["tmZ0"] = bump("tmm")
                e["dZ0"] = bump(f"dZ0{p}")
                lev = []
                for kk in range(1, NLEV + 1):
                    lv = {"tmsq": bump("tmm"), "aXZ": bump(f"aXZ{p}"),
                          "tmpt": bump("tmm"), "dPT": bump(f"dPT{p}")}
                    lev.append(lv)
                e["lev"] = lev
                e["tmWU"] = bump("tmm")
                e["dW"] = bump(f"dW{p}")
                e["tmU"] = bump("tmm")
                e["dU"] = bump(f"dU{p}")
                e["tmO"] = bump("tmm")
                e["aO"] = bump(f"aO{p}")
                e["sS"] = bump(f"sS{p}")
                ev.append(e)
        guard["proj0"] = ("dW1", val("dW1"))
        guard["proj1"] = ("dPT1", val("dPT1"))
        for p in range(2):
            E("rms", tt=tt, p=p, aO_need=val(f"aO{p}"), tmm_need=val("tmm"),
              rms_z=bump("rms_z"),
              rsq=bump("rsq"),
              rms_ms=bump("rms_ms"), rms_r=bump("rms_r"),
              rnorm=bump("rnorm"),
              ost_total=bump("ost", NCH) * 16)
    E("ag2", ost_need=val("ost") * 16)
    return ev, cnt


P2_SEM_NAMES = ["xnt", "tmm", "bsig", "bcast", "bevac", "estash", "esave", "pevac",
                "cdone", "csil", "l2sq", "l2r", "l2rb", "l2d", "kvst", "ktvl",
                "dXL0", "dXL1", "dZ00", "dZ01", "aXZ0", "aXZ1", "dPT0", "dPT1", "dW0", "dW1", "dU0", "dU1", "aO0", "aO1", "sS0", "sS1",
                "rms_z", "rsq", "rms_ms", "rms_r", "rnorm", "ost"]


def p2_alloc(nc, c):
    A = nc.alloc_sbuf_tensor
    off = [0]

    def carve(cols):
        off[0] = (off[0] + 15) // 16 * 16
        ap = c.ar_bf[:, off[0]:off[0] + cols]
        off[0] += cols
        return ap
    c.xnT = carve(16 * TT)
    c.xhat = [carve(3 + TT) for i in range(6)]
    c.cv = [carve(TT) for i in range(6)]
    c.stashB = [A(f"stashB{i}", [128, 8], BF16) for i in range(6)]
    c.sqb = carve(2 * TT)
    c.KT = [carve(2 * TT) for p in range(2)]
    c.QT = [carve(2 * TT) for p in range(2)]
    c.ktv0 = carve(NCH * 512)
    assert off[0] <= 43120, off[0]
    c.brow = c.ar_f32[0:2, 0:TT]
    c.rs_row = c.ar_f32[0:1, TT:2 * TT]
    c.browb = A("browb", [2, TT], BF16)
    c.rs_bf = A("rs_bf", [1, TT], BF16)
    c.bcol = A("bcol", [128, 4 * NCH], F32)
    c.XS, c.X2, c.Z2, c.PT, c.LtS, c.BVK, c.negWT, c.Ub = {}, {}, {}, {}, {}, {}, {}, {}
    for p_ in range(2):
        b0 = 63600 + p_ * 2048
        c.XS[p_] = c.ar_bf[:, b0:b0 + 128]
        c.X2[p_] = [c.ar_bf[:, b0 + 128 + i * 128:b0 + 256 + i * 128] for i in range(2)]
        c.Z2[p_] = [c.ar_bf[:, b0 + 384 + i * 128:b0 + 512 + i * 128] for i in range(2)]
        c.PT[p_] = [c.ar_bf[:, b0 + 640 + i * 128:b0 + 768 + i * 128] for i in range(2)]
        c.LtS[p_] = c.ar_bf[:, b0 + 896:b0 + 1024]
        c.BVK[p_] = c.ar_bf[:, b0 + 1024:b0 + 1536]
        c.negWT[p_] = c.ar_bf[:, b0 + 1536:b0 + 1792]
        c.Ub[p_] = c.ar_bf[:, b0 + 1792:b0 + 2048]
    c.ktv1 = c.ar_bf[:, 57456:57456 + NCH * 512]
    c.ktv = [c.ktv0, c.ktv1]
    c.Otile1 = c.ar_bf[:, 61552:61552 + NCH * 256]
    c.S32 = c.ar_f32[:, 8192:9216]
    c.Sb = A("Sb", [128, 2 * 512], BF16)
    c.Otile = [c.ar_bf[:, 43120:43120 + NCH * 256], c.Otile1]
    c.rmsacc = A("rmsacc", [128, NCH], F32)
    c.rr = A("rr", [128, NCH], F32)
    P = nc.alloc_psum_tensor
    c.pb_proj = [P(f"pb_proj{i}", [128, 512], F32) for i in range(2)]
    c.pb_alw = P("pb_alw", [128, 512], F32)
    c.pb_ser = P("pb_ser", [128, 384], F32)
    c.pb_uo = P("pb_uo", [128, 512], F32)
    c.pb_ds = P("pb_ds", [128, 512], F32)
    c.pb_bf = P("pb_bf", [128, 1024], BF16)
    c.pb_misc = P("pb_misc", [128, 512], F32)


def _pb(c, p):
    if p == 0:
        return c.pb_alw, c.pb_ser, c.pb_uo, (0, 128)
    return c.pb_proj[0], c.pb_proj[1][:, 0:384], c.pb_ds, (384, 512)


def _kq(c, p, which, mb, i):
    buf = c.QT[p] if which == 0 else c.KT[p]
    return buf[:, mb * TT + i * CH: mb * TT + (i + 1) * CH]


def _g(S, eng, g):
    sem_name, v = g
    if v > 0:
        eng.wait_ge(S[sem_name], v)


def p2_sync(c, s, nc, evs):
    S = c.sems
    for e in evs:
        k = e["kind"]
        if k == "xnT":
            if e["wait_tmm"] is not None:
                s.wait_ge(S["tmm"], e["wait_tmm"])
            tt, p = e["tt"], e["p"]
            base = p * 16384 + tt * TT
            for db in range(16):
                s.dma_start(out=c.xnT[:, db * TT:(db + 1) * TT],
                            in_=c.ag1_out[base:base + TT, db * 128:(db + 1) * 128],
                            transpose=True).then_inc(S["xnt"], 16)
        elif k == "ktv":
            tt, p = e["tt"], e["p"]
            s.wait_ge(S["kvst"], e["kvst_need"])
            s.wait_ge(S["tmm"], e["tmm_need"])
            s.wait_ge(S["dXL0"], e["dXL_need"][0])
            s.wait_ge(S["dXL1"], e["dXL_need"][1])
            for i in range(NCH):
                s.dma_start(out=c.ktv[p][:, i * 512:(i + 1) * 512],
                            in_=c.kv_scr[p * 512:(p + 1) * 512,
                                         tt * TT + i * CH: tt * TT + (i + 1) * CH],
                            transpose=True).then_inc(S["ktvl"], 16)


def p2_tensor(c, t, nc, evs):
    S = c.sems
    t.wait_ge(S["p0_cast"], 64)
    t.wait_ge(S["p0_consts"], 16 * 9)
    for e in evs:
        k = e["kind"]
        if k == "beta":
            t.wait_ge(S["xnt"], e["xnt_need"])
            _g(S, t, e["misc_guard"])
            for ch in range(2):
                for db in range(16):
                    mm = t.matmul(c.pb_misc[0:2, 0:512],
                                  c.wbS[:, db * 2:(db + 1) * 2],
                                  c.xnT[:, db * TT + ch * 512: db * TT + (ch + 1) * 512],
                                  start=(db == 0), stop=(db == 15))
                mm.then_inc(S["tmm"], 1)
                if ch == 0:
                    t.wait_ge(S["bsig"], e["bsig"][0])
            t.wait_ge(S["bcast"], e["bcast"])
            _g(S, t, e["bt_guard"])
            for i in range(NCH):
                t.transpose(c.pb_bf[:, 128 + 2 * i:128 + 2 * (i + 1)],
                            c.browb[0:2, i * CH:(i + 1) * CH],
                            c.identS[0:2, 0:2]).then_inc(S["tmm"], 1)
        elif k == "proj":
            p, proj, mb, ch = e["p"], e["proj"], e["mb"], e["ch"]
            t.wait_ge(S["xnt"], e["xnt_need"])
            t.wait_ge(S["estash"], e["estash_need"])
            _g(S, t, e["bank_guard"])
            wS = [c.wqS, c.wkS, c.wvS][proj]
            psum = c.pb_proj[e["bank"]]
            for db in range(16):
                mm = t.matmul(psum[:, 0:512],
                              wS[:, db * 256 + mb * 128: db * 256 + mb * 128 + 128],
                              c.xnT[:, db * TT + ch * 512: db * TT + (ch + 1) * 512],
                              start=(db == 0), stop=(db == 15))
            mm.then_inc(S["tmm"], 1)
        elif k == "l2":
            p, which = e["p"], e["which"]
            t.wait_ge(S["l2sq"], e["l2sq"])
            for ch in range(2):
                d = e["lch"][ch]
                _g(S, t, d["misc_guard"])
                for mb in range(2):
                    mm = t.matmul(c.pb_misc[0:1, 0:512], c.onecS[:],
                                  c.sqb[:, mb * TT + ch * 512: mb * TT + (ch + 1) * 512],
                                  start=(mb == 0), stop=(mb == 1))
                mm.then_inc(S["tmm"], 1)
                t.wait_ge(S["l2rb"], d["l2rb"])
                _g(S, t, d["bank_guard"])
                t.matmul(c.pb_proj[ch][:, 0:512], c.onerS[:],
                         c.rs_bf[0:1, ch * 512:(ch + 1) * 512],
                         start=True, stop=True).then_inc(S["tmm"], 1)
        elif k == "chunk":
            p, i = e["p"], e["i"]
            AL, SER, UO, z0r = _pb(c, p)
            t.wait_ge(S["ktvl"], e["ktvl_need"])
            t.wait_ge(S["l2d"], e["l2d_need"])
            t.wait_ge(S[f"dXL{p}"], e["dXL_prev"])
            for mb in range(2):
                t.matmul(AL[:, 0:128], _kq(c, p, 1, mb, i), _kq(c, p, 1, mb, i),
                         start=(mb == 0), stop=(mb == 1))
            for mb in range(2):
                mm = t.matmul(AL[:, 128:256], _kq(c, p, 1, mb, i), _kq(c, p, 0, mb, i),
                              start=(mb == 0), stop=(mb == 1))
            mm.then_inc(S["tmm"], 1)
            t.wait_ge(S[f"dXL{p}"], e["dXL"])
            t.wait_ge(S[f"dZ0{p}"], e["z0_prev"])
            t.transpose(c.pb_bf[:, z0r[0]:z0r[1]], c.XS[p][:], c.identS[:]).then_inc(S["tmm"], 1)
            t.wait_ge(S[f"dZ0{p}"], e["dZ0"])
            t.wait_ge(S[f"aXZ{p}"], e["aXZ_prev"])
            t.wait_ge(S[f"dPT{p}"], e["dPT_prev"])
            Xcur, Zcur = c.XS[p][:], c.Z2[p][0][:]
            for lv in range(1, NLEV + 1):
                d = e["lev"][lv - 1]
                if lv >= 2:
                    t.wait_ge(S[f"aXZ{p}"], e["lev"][lv - 2]["aXZ"])
                    t.wait_ge(S[f"dPT{p}"], e["lev"][lv - 2]["dPT"])
                mm = t.matmul(SER[:, 0:128], Zcur, Xcur, start=True, stop=True)
                if lv < NLEV:
                    mm = t.matmul(SER[:, 128:256], Xcur, Zcur, start=True, stop=True)
                mm.then_inc(S["tmm"], 1)
                t.wait_ge(S[f"aXZ{p}"], d["aXZ"])
                t.matmul(SER[:, 256:384], c.X2[p][lv % 2][:], c.PT[p][(lv - 1) % 2][:],
                         start=True, stop=True).then_inc(S["tmm"], 1)
                Xcur, Zcur = c.X2[p][lv % 2][:], c.Z2[p][lv % 2][:]
            t.wait_ge(S[f"dPT{p}"], e["lev"][NLEV - 1]["dPT"])
            t.wait_ge(S[f"dW{p}"], e["dW_prev"])
            t.wait_ge(S[f"dU{p}"], e["dU_prev"])
            PTf = c.PT[p][NLEV % 2]
            for mb in range(2):
                t.matmul(AL[:, 256 + mb * 128:256 + (mb + 1) * 128],
                         c.BVK[p][:, mb * 128:(mb + 1) * 128], PTf[:], start=True, stop=True)
            mm = t.matmul(UO[:, 0:256], PTf[:], c.BVK[p][:, 256:512], start=True, stop=False)
            mm.then_inc(S["tmm"], 1)
            t.wait_ge(S[f"dW{p}"], e["dW"])
            t.wait_ge(S[f"sS{p}"], e["sS_need"])
            for mb in range(2):
                mm = t.matmul(UO[:, 0:256], c.negWT[p][:, mb * 128:(mb + 1) * 128],
                              c.Sb[:, p * 512 + mb * 256: p * 512 + (mb + 1) * 256],
                              start=False, stop=(mb == 1))
            mm.then_inc(S["tmm"], 1)
            t.wait_ge(S[f"dU{p}"], e["dU"])
            t.wait_ge(S[f"aO{p}"], e["aO_prev"])
            t.wait_ge(S["sS0"], e["ds_guard"][0])
            t.wait_ge(S["sS1"], e["ds_guard"][1])
            _g(S, t, e["misc_guard"]) if isinstance(e["misc_guard"][1], int) and e["misc_guard"][1] > 0 and e["misc_guard"][0] in ("l2r", "bsig") else None
            for mb in range(2):
                t.matmul(UO[:, 256:512], _kq(c, p, 0, mb, i),
                         c.Sb[:, p * 512 + mb * 256: p * 512 + (mb + 1) * 256],
                         start=(mb == 0), stop=False)
            t.matmul(UO[:, 256:512], c.LtS[p][:], c.Ub[p][:], start=False, stop=True)
            for mb in range(2):
                mm = t.matmul(c.pb_misc[:, mb * 256:(mb + 1) * 256],
                              c.ktv[p][:, i * 512 + mb * 128: i * 512 + (mb + 1) * 128],
                              c.Ub[p][:], start=True, stop=True)
            mm.then_inc(S["tmm"], 1)


def p2_scalar(c, sc, nc, evs, act_inc):
    S = c.sems
    for e in evs:
        k = e["kind"]
        if k == "beta":
            for ch in range(2):
                sc.wait_ge(S["tmm"], e["bmm"][ch])
                sc.activation(out=c.brow[0:2, ch * 512:(ch + 1) * 512],
                              in_=c.pb_misc[0:2, 0:512], func=AF.Sigmoid)
                act_inc(c, sc, S["bsig"])
        elif k == "proj":
            proj, mb, ch = e["proj"], e["mb"], e["ch"]
            sc.wait_ge(S["tmm"], e["pmm"])
            sc.wait_ge(S["esave"], e["esave_need"])
            sc.wait_ge(S["cdone"], e["cdone_need"])
            sc.activation(out=c.xhat[xi(proj, mb)][:, 3 + ch * 512: 3 + (ch + 1) * 512],
                          in_=c.pb_proj[e["bank"]][:, 0:512], func=AF.Copy)
            act_inc(c, sc, S["pevac"])
        elif k == "conv":
            sc.wait_ge(S["cdone"], e["cdone"])
            buf = c.cv[xi(e["proj"], e["mb"])]
            sc.activation(out=buf[:], in_=buf[:], func=AF.Silu)
            act_inc(c, sc, S["csil"])
        elif k == "l2":
            which = e["which"]
            for ch in range(2):
                d = e["lch"][ch]
                sc.wait_ge(S["tmm"], d["l2mm"])
                sc.activation(out=c.rs_row[0:1, ch * 512:(ch + 1) * 512],
                              in_=c.pb_misc[0:1, 0:512], func=AF.Ln, bias=1e-6)
                sc.drain()
                if which == 0:
                    sc.activation(out=c.rs_row[0:1, ch * 512:(ch + 1) * 512],
                                  in_=c.rs_row[0:1, ch * 512:(ch + 1) * 512],
                                  func=AF.Exp, scale=-0.5, bias=LN_DK)
                else:
                    sc.activation(out=c.rs_row[0:1, ch * 512:(ch + 1) * 512],
                                  in_=c.rs_row[0:1, ch * 512:(ch + 1) * 512],
                                  func=AF.Exp, scale=-0.5)
                act_inc(c, sc, S["l2r"])
        elif k == "chunk":
            p = e["p"]
            AL, SER, UO, z0r = _pb(c, p)
            for lv in range(1, NLEV + 1):
                d = e["lev"][lv - 1]
                sc.wait_ge(S["tmm"], d["tmsq"])
                sc.activation(out=c.X2[p][lv % 2][:], in_=SER[:, 0:128], func=AF.Copy)
                if lv < NLEV:
                    sc.drain()
                    sc.activation(out=c.Z2[p][lv % 2][:], in_=SER[:, 128:256], func=AF.Copy)
                act_inc(c, sc, S[f"aXZ{p}"])
            sc.wait_ge(S["tmm"], e["tmO"])
            sc.wait_ge(S["ost"], e["ost_prev"])
            sc.activation(out=c.Otile[p][:, e["i"] * 256:(e["i"] + 1) * 256],
                          in_=UO[:, 256:512], func=AF.Copy)
            act_inc(c, sc, S[f"aO{p}"])
        elif k == "rms":
            sc.wait_ge(S[f"aO{e['p']}"], e["aO_need"])
            sc.wait_ge(S["tmm"], e["tmm_need"])
            sc.wait_ge(S["rms_z"], e["rms_z"])
            for i in range(NCH):
                sc.activation(out=c.sqb[:, 0:256], in_=c.Otile[e["p"]][:, i * 256:(i + 1) * 256],
                              func=AF.Square, accum_out=c.rmsacc[:, i:i + 1])
                sc.drain()
            act_inc(c, sc, S["rsq"])
            sc.wait_ge(S["rms_ms"], e["rms_ms"])
            sc.activation(out=c.rr[:], in_=c.rr[:], func=AF.Ln, bias=1e-5)
            sc.drain()
            sc.activation(out=c.rr[:], in_=c.rr[:], func=AF.Exp, scale=-0.5)
            act_inc(c, sc, S["rms_r"])


def p2_vector(c, v, nc, evs, dve_inc):
    S = c.sems
    for e in evs:
        k = e["kind"]
        if k == "beta":
            v.wait_ge(S["bsig"], e["bsig"][1])
            v.tensor_copy(out=c.browb[0:2, :], in_=c.brow[0:2, :])
            dve_inc(v, S["bcast"])
            v.wait_ge(S["tmm"], e["btr"][NCH - 1])
            v.tensor_copy(out=c.bcol[:, e["p"] * 16:e["p"] * 16 + 16],
                          in_=c.pb_bf[:, 128:128 + 2 * NCH])
            dve_inc(v, S["bevac"])
        elif k == "stash":
            tt, p = e["tt"], e["p"]
            v.wait_ge(S["cdone"], e["cdone_need"])
            v.wait_ge(S["pevac"], e["pevac_need"])
            for b in range(6):
                if tt == 0 and p == 0:
                    v.memset(c.stashB[b][:], 0.0)
                    v.drain()
                v.tensor_copy(out=c.xhat[b][:, 0:3], in_=c.stashB[b][:, p * 4:p * 4 + 3])
            dve_inc(v, S["estash"])
        elif k == "stash_save":
            p = e["p"]
            v.wait_ge(S["pevac"], e["pevac_need"])
            for b in range(6):
                v.tensor_copy(out=c.stashB[b][:, p * 4:p * 4 + 3], in_=c.xhat[b][:, TT:TT + 3])
            dve_inc(v, S["esave"])
        elif k == "conv":
            p, proj, mb = e["p"], e["proj"], e["mb"]
            v.wait_ge(S["pevac"], e["pevac_need"])
            v.wait_ge(S["esave"], e["esave_need"])
            v.wait_ge(S["kvst"], e["kvst_guard"])
            xb = c.xhat[xi(proj, mb)]
            out = c.cv[xi(proj, mb)]
            base = proj * 16 + (p * 2 + mb) * 4
            v.tensor_scalar(out=out[:], in0=xb[:, 3:3 + TT],
                            scalar1=c.convS[:, base + 3:base + 4], scalar2=None, op0=ALU.mult)
            for j in (2, 1, 0):
                v.drain()
                v.scalar_tensor_tensor(out=out[:], in0=xb[:, j:j + TT],
                                       scalar=c.convS[:, base + j:base + j + 1],
                                       in1=out[:], op0=ALU.mult, op1=ALU.add)
            dve_inc(v, S["cdone"])
        elif k == "l2":
            p, which = e["p"], e["which"]
            v.wait_ge(S["csil"], e["csil_need"])
            for mb in range(2):
                buf = c.cv[xi(which, mb)]
                v.tensor_mul(out=c.sqb[:, mb * TT:(mb + 1) * TT], in0=buf[:], in1=buf[:])
            dve_inc(v, S["l2sq"])
            dst = (c.QT if which == 0 else c.KT)[p]
            for ch in range(2):
                d = e["lch"][ch]
                v.wait_ge(S["l2r"], d["l2r"])
                v.tensor_copy(out=c.rs_bf[0:1, ch * 512:(ch + 1) * 512],
                              in_=c.rs_row[0:1, ch * 512:(ch + 1) * 512])
                dve_inc(v, S["l2rb"])
                v.wait_ge(S["tmm"], d["l2bc"])
                v.wait_ge(S["kvst"], e["kvst_guard"])
                for mb in range(2):
                    v.tensor_mul(out=dst[:, mb * TT + ch * 512: mb * TT + (ch + 1) * 512],
                                 in0=c.cv[xi(which, mb)][:, ch * 512:(ch + 1) * 512],
                                 in1=c.pb_proj[ch][:, 0:512])
            dve_inc(v, S["l2d"])
        elif k == "sinit":
            p = e["p"]
            v.memset(c.S32[:, p * 512:(p + 1) * 512], 0.0)
            v.memset(c.Sb[:, p * 512:(p + 1) * 512], 0.0)
            dve_inc(v, S[f"sS{p}"])
        elif k == "chunk":
            p, i = e["p"], e["i"]
            AL, SER, UO, z0r = _pb(c, p)
            v.wait_ge(S["tmm"], e["tmA"])
            v.wait_ge(S["ktvl"], e["ktvl_need"])
            v.wait_ge(S["bevac"], e["bevac_need"])
            v.scalar_tensor_tensor(out=c.XS[p][:], in0=AL[:, 0:128],
                                   scalar=c.bcol[:, p * 16 + i * 2: p * 16 + i * 2 + 1],
                                   in1=c.negstrilS[:], op0=ALU.mult, op1=ALU.mult)
            v.drain()
            v.tensor_mul(out=c.LtS[p][:], in0=AL[:, 128:256], in1=c.triuS[:])
            v.drain()
            v.tensor_scalar(out=c.BVK[p][:], in0=c.ktv[p][:, i * 512:(i + 1) * 512],
                            scalar1=c.bcol[:, p * 16 + i * 2: p * 16 + i * 2 + 1], scalar2=None,
                            op0=ALU.mult)
            dve_inc(v, S[f"dXL{p}"])
            v.wait_ge(S["tmm"], e["tmZ0"])
            v.tensor_copy(out=c.Z2[p][0][:], in_=c.pb_bf[:, z0r[0]:z0r[1]])
            v.drain()
            v.tensor_add(out=c.PT[p][0][:], in0=c.Z2[p][0][:], in1=c.identS[:])
            dve_inc(v, S[f"dZ0{p}"])
            for lv in range(1, NLEV + 1):
                d = e["lev"][lv - 1]
                v.wait_ge(S["tmm"], d["tmpt"])
                v.tensor_add(out=c.PT[p][lv % 2][:], in0=c.PT[p][(lv - 1) % 2][:],
                             in1=SER[:, 256:384])
                dve_inc(v, S[f"dPT{p}"])
            v.wait_ge(S["tmm"], e["tmWU"])
            v.tensor_scalar(out=c.negWT[p][:], in0=AL[:, 256:512], scalar1=-1.0,
                            scalar2=None, op0=ALU.mult)
            dve_inc(v, S[f"dW{p}"])
            v.wait_ge(S["tmm"], e["tmU"])
            v.tensor_copy(out=c.Ub[p][:], in_=UO[:, 0:256])
            dve_inc(v, S[f"dU{p}"])
            v.wait_ge(S["tmm"], e["tmO"])
            for mb in range(2):
                v.tensor_add(out=c.S32[:, p * 512 + mb * 256: p * 512 + (mb + 1) * 256],
                             in0=c.S32[:, p * 512 + mb * 256: p * 512 + (mb + 1) * 256],
                             in1=c.pb_misc[:, mb * 256:(mb + 1) * 256])
                v.drain()
            v.tensor_copy(out=c.Sb[:, p * 512:(p + 1) * 512], in_=c.S32[:, p * 512:(p + 1) * 512])
            dve_inc(v, S[f"sS{p}"])
            if False:
                v.drain()
                v.tensor_copy(out=c.dbgC[:, 0:128], in_=c.XS[:])
                v.tensor_copy(out=c.dbgC[:, 128:256], in_=c.LtS[:])
                v.tensor_copy(out=c.dbgC[:, 256:384], in_=c.PT[NLEV % 2][:])
                v.tensor_copy(out=c.dbgC[:, 384:896], in_=c.BVK[:])
                v.tensor_copy(out=c.dbgC[:, 896:1152], in_=c.Ub[:])
                v.tensor_copy(out=c.dbgC[:, 1152:1280], in_=c.QT[0][:, 0:128])
                v.tensor_copy(out=c.dbgC[:, 1280:1408], in_=c.QT[0][:, TT:TT + 128])
                dve_inc(v, S["dbgc"])
        elif k == "rms":
            v.memset(c.rmsacc[:], 0.0)
            dve_inc(v, S["rms_z"])
            v.wait_ge(S["rsq"], e["rsq"])
            v.tensor_scalar(out=c.rr[:], in0=c.rmsacc[:], scalar1=1.0 / 256, scalar2=None,
                            op0=ALU.mult)
            dve_inc(v, S["rms_ms"])
            v.wait_ge(S["rms_r"], e["rms_r"])
            for i in range(NCH):
                v.scalar_tensor_tensor(out=c.Otile[e["p"]][:, i * 256:(i + 1) * 256],
                                       in0=c.Otile[e["p"]][:, i * 256:(i + 1) * 256],
                                       scalar=c.rr[:, i:i + 1], in1=c.normwS[:],
                                       op0=ALU.mult, op1=ALU.mult)
            dve_inc(v, S["rnorm"])


def p2_gpsimd(c, g, nc, evs, ag_groups):
    S = c.sems
    for e in evs:
        k = e["kind"]
        if k == "kvstore":
            tt, p = e["tt"], e["p"]
            g.wait_ge(S["l2d"], e["l2d_need"])
            g.wait_ge(S["csil"], e["csil_need"])
            for mb in range(2):
                g.dma_start(out=c.kv_scr[p * 512 + mb * 128: p * 512 + (mb + 1) * 128,
                                         tt * TT:(tt + 1) * TT],
                            in_=c.KT[p][:, mb * TT:(mb + 1) * TT]).then_inc(S["kvst"], 16)
                g.dma_start(out=c.kv_scr[p * 512 + 256 + mb * 128: p * 512 + 256 + (mb + 1) * 128,
                                         tt * TT:(tt + 1) * TT],
                            in_=c.cv[xi(2, mb)][:]).then_inc(S["kvst"], 16)
        elif k == "rms":
            tt, p = e["tt"], e["p"]
            g.wait_ge(S["rnorm"], e["rnorm"])
            for i in range(NCH):
                t_ = tt * TT + i * CH
                row = (p * 4 + t_ // 4096) * 4096 + (t_ % 4096)
                g.dma_start(out=c.ag2_in[row:row + CH, 0:256],
                            in_=c.Otile[e["p"]][:, i * 256:(i + 1) * 256]).then_inc(S["ost"], 16)
        elif k == "ag2":
            g.wait_ge(S["ost"], e["ost_need"])
            from concourse.aot_env import allow_ici_all_to_all
            with allow_ici_all_to_all():
                g.collective_compute("AllToAll", ALU.bypass, replica_groups=ag_groups,
                                     ins=[c.ag2_in.ap()], outs=[c.ag2_out.ap()]).then_inc(S["cc"])



GRP = 512
NG = 8
NTB = 4


def build_p3_events():
    ev = []
    cnt = {}

    def bump(k, n=1):
        cnt[k] = cnt.get(k, 0) + n
        return cnt[k]

    def val(k):
        return cnt.get(k, 0)

    def E(kind, **kw):
        kw["kind"] = kind
        ev.append(kw)
        return kw

    w13c = [0, 0, 0, 0]
    w13_guard = [0, 0, 0, 0]
    w2c = [0, 0]
    w2_guard = [0, 0]
    x1c = [0, 0]
    outc = [0, 0]
    hist = [0, 0]

    E("woload", wo_total=bump("p3wo", 16) * 16)
    for g in range(NG):
        E("oT", g=g, tmm_guard=val("tmm3"), ln2_guard=val("aLn"),
          hist_guard=(hist[0] * 48, hist[1] * 48),
          oTl=bump("oTl", 16) * 16, xld=bump("xld", 4) * 16)
        for tb in range(NTB):
            for dchunk in range(4):
                E("wo", g=g, tb=tb, dchunk=dchunk, bank=dchunk % 2,
                  oTl_need=val("oTl") * 16, wo_need=val("p3wo") * 16,
                  xld_need=val("xld") * 16,
                  bank_guard=max(val("aWo") - 1, 0),
                  tmm=bump("tmm3"), aWo=bump("aWo"))
            par = (g * NTB + tb) % 2
            e = E("ln2", g=g, tb=tb, par=par, aWo_need=val("aWo"),
                  hist_guard=hist[par] * 48,
                  sq=bump("sq3"), dv=bump("dv3"), st1=bump("st3"),
                  aLn=bump("aLn"))
            hist[par] += 1
            e["hist_total"] = hist[par] * 48
    for g in range(NG):
        E("xfT", g=g, hist_need=(hist[0] * 48, hist[1] * 48),
          tmm_guard=val("tmm3"), rin_guard=val("rin"), gate_guard=val("tmm3"),
          xfl=bump("xfl", 32) * 16)
        for tb in range(NTB):
            E("gate", g=g, tb=tb, xfl_need=val("xfl") * 16,
              gpsum_guard=val("aSig"), rin_guard=val("rin"),
              mskp_guard=val("mskst") * 16,
              tmm=bump("tmm3"), aSig=bump("aSig"), gdone=bump("gdone"),
              tmsk=bump("tmm3"), mskd=bump("mskd"),
              mskst=bump("mskst") * 16)
        rounds = [list(range(4)), list(range(4, 8)), [8]]
        for r, exps in enumerate(rounds):
            for e_ in exps:
                if e_ < 8:
                    E("mskb", g=g, e=e_, mskst_need=val("mskst") * 16,
                      rin_guard=val("rin"), aSig_guard=val("aSig"),
                      mskrl=bump("mskrl") * 16,
                      tmm=bump("tmm3"), rin=bump("rin"))
                for hb in range(8):
                    s1 = (val("w13ev") * 2) % 4
                    s3 = s1 + 1
                    bump("w13ev")
                    w13c[s1] += 1
                    w13c[s3] += 1
                    d = E("ffnh", g=g, e=e_, hb=hb, s1=s1, s3=s3,
                          rin_need=val("rin"),
                          w1_need=w13c[s1] * 16, w3_need=w13c[s3] * 16,
                          slot_guard=(w13_guard[s1], w13_guard[s3]),
                          aSil_guard=val("aSil"), hdone_guard=val("hdone"),
                          ydone_guard=val("ydone"),
                          tmm_h1=bump("tmm3"), aSil=bump("aSil"),
                          tmm_h3=bump("tmm3"), hdone=bump("hdone"))
                    w13_guard[s1] = d["tmm_h1"]
                    w13_guard[s3] = d["tmm_h3"]
            for dchunk in range(4):
                for si, e_ in enumerate(exps):
                    sl = val("w2ev") % 2
                    bump("w2ev")
                    w2c[sl] += 8
                    d = E("ffny", g=g, r=r, dchunk=dchunk, si=si, e=e_, slot=sl,
                          nsub=len(exps),
                          hdone_need=val("hdone"),
                          w2_need=w2c[sl] * 16, slot_guard=w2_guard[sl],
                          y_guard=val("ydone"),
                          tmm=bump("tmm3"),
                          first_round=(r == 0))
                    w2_guard[sl] = d["tmm"]
                    if si == len(exps) - 1:
                        d["ydone"] = bump("ydone")
                    else:
                        d["ydone"] = None
        for tb in range(NTB):
            buf = (g * NTB + tb) % 2
            x1c[buf] += 1
            e = E("fout", g=g, tb=tb, buf=buf, ydone_need=val("ydone"),
                  out_guard=outc[buf] * 16,
                  x1_need=x1c[buf] * 16,
                  odone=bump("odone"))
            outc[buf] += 1
            e["out_total"] = outc[buf] * 16
    E("p3fin", out_totals=(outc[0] * 16, outc[1] * 16))
    return ev, cnt


P3_SEM_NAMES = ["p3wo", "oTl", "xld", "tmm3", "aWo", "sq3", "dv3", "st3", "aLn",
                "histn0", "histn1", "xfl", "aSig", "gdone", "mskd", "rin",
                "w13l0", "w13l1", "w13l2", "w13l3", "aSil", "hdone", "w2l0", "w2l1",
                "ydone", "x1l0", "x1l1", "odone", "outst0", "outst1", "mskst", "mskrl"]


def p3_alloc(nc, c):
    bf = c.ar_bf
    f32 = c.ar_f32
    c.woALL = bf[:, 0:32768]
    c.oTt = bf[:, 32768:40960]
    c.hi_stg = [bf[:, 40960 + i * 2048:40960 + (i + 1) * 2048] for i in range(2)]
    c.lo_stg = [bf[:, 45056 + i * 2048:45056 + (i + 1) * 2048] for i in range(2)]
    c.sqj3 = bf[:, 49152:51200]
    c.x_g = f32[:, 0:8192]
    c.xf3 = f32[:, 8192:10240]
    c.hiT = bf[:, 0:8192]
    c.loT = bf[:, 8192:16384]
    c.rinT = bf[:, 8192:16384]
    c.hT = [bf[:, 16384 + i * 4096:16384 + (i + 1) * 4096] for i in range(4)]
    c.w13buf = [bf[:, 32768 + i * 2048:32768 + (i + 1) * 2048] for i in range(4)]
    c.w2D = [bf[:, 40960 + i * 4096:40960 + (i + 1) * 4096] for i in range(2)]
    c.hsil = bf[:, 49152:49664]
    c.y_acc = bf[:, 49664:57856]
    c.mskT8 = bf[0:8, 57856:57984]
    c.mskrow = bf[0:1, 57984:58496]
    c.mskbc = bf[:, 61952:62464]
    c.mskb8 = bf[:, 62464:62472]
    c.x1b = [f32[:, i * 2048:(i + 1) * 2048] for i in range(2)]
    c.sc8 = [f32[:, 4096 + i * 16:4096 + i * 16 + 8] for i in range(6)]
    c.m2t = f32[:, 4192:4200]
    c.r1t = f32[:, 4208:4216]


def w13_block(e, wtype):
    if e == 8:
        return 16 if wtype == 0 else 17
    return 2 * e + wtype


def p3_sync(c, s, nc, evs):
    S = c.sems
    s.wait_ge(S["cc"], 4)
    for e in evs:
        k = e["kind"]
        if k == "woload":
            for db in range(16):
                s.dma_start(out=c.woALL[:, db * 2048:(db + 1) * 2048],
                            in_=c.wo_scr[db * 128:(db + 1) * 128, :]).then_inc(S["p3wo"], 16)
        elif k == "oT":
            g = e["g"]
            s.wait_ge(S["tmm3"], e["tmm_guard"])
            s.wait_ge(S["aLn"], e["ln2_guard"])
            s.wait_ge(S["histn0"], e["hist_guard"][0])
            s.wait_ge(S["histn1"], e["hist_guard"][1])
            t0 = g * GRP
            for db in range(16):
                h = db // 2
                s.dma_start(out=c.oTt[:, db * 512:(db + 1) * 512],
                            in_=c.ag2_out[h * 4096 + t0:h * 4096 + t0 + GRP,
                                          (db % 2) * 128:(db % 2) * 128 + 128],
                            transpose=True).then_inc(S["oTl"], 16)
            for tb in range(NTB):
                s.dma_start(out=c.x_g[:, tb * 2048:(tb + 1) * 2048],
                            in_=c.x[t0 + tb * 128:t0 + (tb + 1) * 128, :]).then_inc(S["xld"], 16)
        elif k == "xfT":
            g = e["g"]
            s.wait_ge(S["histn0"], e["hist_need"][0])
            s.wait_ge(S["histn1"], e["hist_need"][1])
            s.wait_ge(S["tmm3"], e["tmm_guard"])
            s.wait_ge(S["rin"], e["rin_guard"])
            t0 = g * GRP
            for db in range(16):
                s.dma_start(out=c.hiT[:, db * 512:(db + 1) * 512],
                            in_=c.xfhi_scr[t0:t0 + GRP, db * 128:(db + 1) * 128],
                            transpose=True).then_inc(S["xfl"], 16)
            for db in range(16):
                s.dma_start(out=c.loT[:, db * 512:(db + 1) * 512],
                            in_=c.xflo_scr[t0:t0 + GRP, db * 128:(db + 1) * 128],
                            transpose=True).then_inc(S["xfl"], 16)
        elif k == "mskb":
            s.wait_ge(S["mskst"], e["mskst_need"])
            s.wait_ge(S["rin"], e["rin_guard"])
            s.dma_start(out=c.mskrow[0:1, :],
                        in_=c.msk_scr[e["e"]:e["e"] + 1, e["g"] * GRP:(e["g"] + 1) * GRP]
                        ).then_inc(S["mskrl"], 16)
        elif k == "ffnh":
            s.wait_ge(S["tmm3"], max(e["slot_guard"]))
            b1 = w13_block(e["e"], 0)
            b3 = w13_block(e["e"], 1)
            hb = e["hb"]
            s.dma_start(out=c.w13buf[e["s1"]][:],
                        in_=c.agw13_out[(b1 * 8 + hb) * 128:(b1 * 8 + hb + 1) * 128, :]
                        ).then_inc(S[f"w13l{e['s1']}"], 16)
            s.dma_start(out=c.w13buf[e["s3"]][:],
                        in_=c.agw13_out[(b3 * 8 + hb) * 128:(b3 * 8 + hb + 1) * 128, :]
                        ).then_inc(S[f"w13l{e['s3']}"], 16)
        elif k == "ffny":
            s.wait_ge(S["tmm3"], e["slot_guard"])
            base = e["e"] * 1024
            dchunk = e["dchunk"]
            for hb in range(8):
                s.dma_start(out=c.w2D[e["slot"]][:, hb * 512:(hb + 1) * 512],
                            in_=c.agw2_out[base + hb * 128:base + (hb + 1) * 128,
                                           dchunk * 512:(dchunk + 1) * 512]
                            ).then_inc(S[f"w2l{e['slot']}"], 16)
        elif k == "fout":
            g, tb, buf = e["g"], e["tb"], e["buf"]
            s.wait_ge(S["ydone"], e["ydone_need"])
            s.wait_ge(S[f"outst{buf}"], e["out_guard"])
            s.dma_start(out=c.x1b[buf][:],
                        in_=c.x1_scr[g * GRP + tb * 128:g * GRP + (tb + 1) * 128, :]
                        ).then_inc(S[f"x1l{buf}"], 16)


def p3_tensor(c, t, nc, evs):
    S = c.sems
    t.wait_ge(S["cc"], 4)
    t.wait_ge(S["p0_cast"], 80)
    psy = [c.pb_proj[0], c.pb_proj[1], c.pb_uo, c.pb_ds]
    for e in evs:
        k = e["kind"]
        if k == "wo":
            t.wait_ge(S["oTl"], e["oTl_need"])
            t.wait_ge(S["p3wo"], e["wo_need"])
            t.wait_ge(S["aWo"], e["bank_guard"])
            tb, dchunk = e["tb"], e["dchunk"]
            psum = c.pb_proj[e["bank"]]
            for db in range(16):
                mm = t.matmul(psum[:, 0:512],
                              c.oTt[:, db * 512 + tb * 128: db * 512 + tb * 128 + 128],
                              c.woALL[:, db * 2048 + dchunk * 512: db * 2048 + (dchunk + 1) * 512],
                              start=(db == 0), stop=(db == 15))
            mm.then_inc(S["tmm3"], 1)
        elif k == "gate":
            tb = e["tb"]
            t.wait_ge(S["xfl"], e["xfl_need"])
            t.wait_ge(S["aSig"], e["gpsum_guard"])
            t.wait_ge(S["rin"], e["rin_guard"])
            first = True
            for db in range(16):
                hiT = c.hiT[:, db * 512 + tb * 128: db * 512 + tb * 128 + 128]
                loT = c.loT[:, db * 512 + tb * 128: db * 512 + tb * 128 + 128]
                ghi = c.gwhiS[:, db * 8:(db + 1) * 8]
                glo = c.gwloS[:, db * 8:(db + 1) * 8]
                t.matmul(c.pb_misc[:, 0:8], hiT, ghi, start=first, stop=False)
                first = False
                t.matmul(c.pb_misc[:, 0:8], loT, ghi, start=False, stop=False)
                mm = t.matmul(c.pb_misc[:, 0:8], hiT, glo, start=False, stop=(db == 15))
            mm.then_inc(S["tmm3"], 1)
            t.wait_ge(S["gdone"], e["gdone"])
            t.wait_ge(S["mskd"], e["mskd"] - 1)
            t.transpose(c.pb_bf[0:8, 256:384], c.mskb8[:], c.identS[:]).then_inc(S["tmm3"], 1)
        elif k == "mskb":
            t.wait_ge(S["mskrl"], e["mskrl"])
            t.wait_ge(S["rin"], e["rin_guard"])
            t.wait_ge(S["aSig"], e["aSig_guard"])
            t.matmul(c.pb_misc[:, 0:512], c.onerS[:], c.mskrow[0:1, :],
                     start=True, stop=True).then_inc(S["tmm3"], 1)
        elif k == "ffnh":
            rin = c.hiT if e["e"] == 8 else c.rinT
            t.wait_ge(S["rin"], e["rin_need"])
            t.wait_ge(S[f"w13l{e['s1']}"], e["w1_need"])
            t.wait_ge(S[f"w13l{e['s3']}"], e["w3_need"])
            t.wait_ge(S["aSil"], e["aSil_guard"])
            t.wait_ge(S["hdone"], e["hdone_guard"])
            t.wait_ge(S["ydone"], e["ydone_guard"])
            for db in range(16):
                mm = t.matmul(c.pb_alw[:, 0:512], c.w13buf[e["s1"]][:, db * 128:(db + 1) * 128],
                              rin[:, db * 512:(db + 1) * 512], start=(db == 0), stop=(db == 15))
            mm.then_inc(S["tmm3"], 1)
            for db in range(16):
                mm = t.matmul(c.pb_uo[:, 0:512], c.w13buf[e["s3"]][:, db * 128:(db + 1) * 128],
                              rin[:, db * 512:(db + 1) * 512], start=(db == 0), stop=(db == 15))
            mm.then_inc(S["tmm3"], 1)
        elif k == "ffny":
            t.wait_ge(S["hdone"], e["hdone_need"])
            t.wait_ge(S[f"w2l{e['slot']}"], e["w2_need"])
            if e["si"] == 0:
                t.wait_ge(S["ydone"], e["y_guard"])
            hTe = c.hT[e["e"] % 4] if e["e"] < 8 else c.hT[0]
            for tb in range(NTB):
                for hb in range(8):
                    mm = t.matmul(psy[tb][:, 0:512],
                                  hTe[:, hb * 512 + tb * 128: hb * 512 + tb * 128 + 128],
                                  c.w2D[e["slot"]][:, hb * 512:(hb + 1) * 512],
                                  start=(e["si"] == 0 and hb == 0),
                                  stop=(e["si"] == e["nsub"] - 1 and hb == 7))
            mm.then_inc(S["tmm3"], 1)


def p3_scalar(c, sc, nc, evs, act_inc):
    S = c.sems
    sc.wait_ge(S["cc"], 4)
    for e in evs:
        k = e["kind"]
        if k == "ln2":
            sc.wait_ge(S["aWo"], e["aWo_need"])
            x1 = c.x_g[:, e["tb"] * 2048:(e["tb"] + 1) * 2048]
            sc.activation(out=c.sqj3[:], in_=x1, func=AF.Square, accum_out=c.stats[:, 1:2])
            act_inc(c, sc, S["sq3"])
            sc.wait_ge(S["dv3"], e["dv"])
            sc.activation(out=c.stats[:, 6:7], in_=c.stats[:, 3:4], func=AF.Ln, bias=1e-5)
            sc.drain()
            sc.activation(out=c.stats[:, 4:5], in_=c.stats[:, 6:7], func=AF.Exp, scale=-0.5)
            act_inc(c, sc, S["st3"])
        elif k == "gate":
            sc.wait_ge(S["tmm3"], e["tmm"])
            sc.wait_ge(S["gdone"], e["gdone"] - 1)
            sc.activation(out=c.sc8[0], in_=c.pb_misc[:, 0:8], func=AF.Sigmoid)
            act_inc(c, sc, S["aSig"])
        elif k == "ffnh":
            sc.wait_ge(S["tmm3"], e["tmm_h1"])
            sc.wait_ge(S["hdone"], e["hdone_guard"])
            sc.activation(out=c.hsil[:], in_=c.pb_alw[:, 0:512], func=AF.Silu)
            act_inc(c, sc, S["aSil"])


def p3_vector(c, v, nc, evs, dve_inc):
    S = c.sems
    v.wait_ge(S["cc"], 4)
    for e in evs:
        k = e["kind"]
        if k == "wo":
            v.wait_ge(S["tmm3"], e["tmm"])
            v.wait_ge(S["xld"], e["xld_need"])
            tb, dchunk = e["tb"], e["dchunk"]
            sl = c.x_g[:, tb * 2048 + dchunk * 512: tb * 2048 + (dchunk + 1) * 512]
            v.tensor_add(out=sl, in0=sl, in1=c.pb_proj[e["bank"]][:, 0:512])
            dve_inc(v, S["aWo"])
        elif k == "ln2":
            tb, par = e["tb"], e["par"]
            x1 = c.x_g[:, tb * 2048:(tb + 1) * 2048]
            v.wait_ge(S["aWo"], e["aWo_need"])
            v.reduce_sum(out=c.stats[:, 0:1], in_=x1, axis=AX.X)
            v.drain()
            v.tensor_scalar(out=c.stats[:, 2:3], in0=c.stats[:, 0:1], scalar1=1.0 / 2048,
                            scalar2=None, op0=ALU.mult)
            v.drain()
            v.tensor_mul(out=c.stats[:, 5:6], in0=c.stats[:, 2:3], in1=c.stats[:, 2:3])
            v.drain()
            v.wait_ge(S["sq3"], e["sq"])
            v.tensor_scalar(out=c.stats[:, 3:4], in0=c.stats[:, 1:2], scalar1=1.0 / 2048,
                            scalar2=c.stats[:, 5:6], op0=ALU.mult, op1=ALU.subtract)
            dve_inc(v, S["dv3"])
            v.wait_ge(S["st3"], e["st1"])
            if e["hist_guard"]:
                v.wait_ge(S[f"histn{par}"], e["hist_guard"])
            v.tensor_scalar(out=c.xf3[:], in0=x1, scalar1=c.stats[:, 2:3],
                            scalar2=c.stats[:, 4:5], op0=ALU.subtract, op1=ALU.mult)
            v.drain()
            v.tensor_copy(out=c.hi_stg[par][:], in_=c.xf3[:])
            v.drain()
            v.tensor_sub(out=c.lo_stg[par][:], in0=c.xf3[:], in1=c.hi_stg[par][:])
            dve_inc(v, S["aLn"])
        elif k == "gate":
            g, tb = e["g"], e["tb"]
            v.wait_ge(S["aSig"], e["aSig"])
            scores, vv, eq, cs, cs2, m1 = c.sc8
            m2 = c.m2t
            v.tensor_add(out=vv, in0=scores, in1=c.biasbS[:])
            v.drain()

            def argmax_mask(vin, mout):
                v.reduce_max(out=c.r1t[:, 0:1], in_=vin, axis=AX.X)
                v.drain()
                v.tensor_scalar(out=eq, in0=vin, scalar1=c.r1t[:, 0:1], scalar2=None,
                                op0=ALU.is_ge)
                v.drain()
                v.tensor_copy(out=cs, in_=eq)
                v.drain()
                v.tensor_copy(out=cs2[:, 0:1], in_=cs[:, 0:1])
                v.tensor_add(out=cs2[:, 1:8], in0=cs[:, 1:8], in1=cs[:, 0:7])
                v.drain()
                v.tensor_copy(out=cs[:, 0:2], in_=cs2[:, 0:2])
                v.tensor_add(out=cs[:, 2:8], in0=cs2[:, 2:8], in1=cs2[:, 0:6])
                v.drain()
                v.tensor_copy(out=cs2[:, 0:4], in_=cs[:, 0:4])
                v.tensor_add(out=cs2[:, 4:8], in0=cs[:, 4:8], in1=cs[:, 0:4])
                v.drain()
                v.tensor_scalar(out=cs, in0=cs2, scalar1=1.5, scalar2=None, op0=ALU.is_le)
                v.drain()
                v.tensor_mul(out=mout, in0=eq, in1=cs)
                v.drain()
            argmax_mask(vv, m1)
            v.scalar_tensor_tensor(out=vv, in0=m1, scalar=-1e9, in1=vv,
                                   op0=ALU.mult, op1=ALU.add)
            v.drain()
            argmax_mask(vv, m2)
            v.tensor_add(out=m1, in0=m1, in1=m2)
            v.drain()
            v.tensor_mul(out=m1, in0=m1, in1=scores)
            v.drain()
            v.tensor_copy(out=c.mskb8[:], in_=m1)
            dve_inc(v, S["gdone"])
            v.wait_ge(S["tmm3"], e["tmsk"])
            v.wait_ge(S["mskst"], e["mskp_guard"])
            v.tensor_copy(out=c.mskT8[0:8, :], in_=c.pb_bf[0:8, 256:384])
            dve_inc(v, S["mskd"])
        elif k == "mskb":
            v.wait_ge(S["tmm3"], e["tmm"])
            v.tensor_copy(out=c.mskbc[:], in_=c.pb_misc[:, 0:512])
            v.drain()
            for db in range(16):
                v.tensor_mul(out=c.rinT[:, db * 512:(db + 1) * 512],
                             in0=c.hiT[:, db * 512:(db + 1) * 512], in1=c.mskbc[:])
            dve_inc(v, S["rin"])
        elif k == "ffnh":
            v.wait_ge(S["aSil"], e["aSil"])
            v.wait_ge(S["tmm3"], e["tmm_h3"])
            hTe = c.hT[e["e"] % 4] if e["e"] < 8 else c.hT[0]
            hb = e["hb"]
            v.tensor_mul(out=hTe[:, hb * 512:(hb + 1) * 512], in0=c.hsil[:],
                         in1=c.pb_uo[:, 0:512])
            dve_inc(v, S["hdone"])
        elif k == "ffny":
            if e["ydone"] is None:
                continue
            v.wait_ge(S["tmm3"], e["tmm"])
            psy = [c.pb_proj[0], c.pb_proj[1], c.pb_uo, c.pb_ds]
            dchunk = e["dchunk"]
            for tb in range(NTB):
                dst = c.y_acc[:, tb * 2048 + dchunk * 512: tb * 2048 + (dchunk + 1) * 512]
                if e["first_round"]:
                    v.tensor_copy(out=dst, in_=psy[tb][:, 0:512])
                else:
                    v.tensor_add(out=dst, in0=dst, in1=psy[tb][:, 0:512])
            dve_inc(v, S["ydone"])
        elif k == "fout":
            buf = e["buf"]
            v.wait_ge(S[f"x1l{buf}"], e["x1_need"])
            v.wait_ge(S["ydone"], e["ydone_need"])
            v.tensor_add(out=c.x1b[buf][:], in0=c.x1b[buf][:],
                         in1=c.y_acc[:, e["tb"] * 2048:(e["tb"] + 1) * 2048])
            dve_inc(v, S["odone"])


def p3_gpsimd(c, g_, nc, evs):
    S = c.sems
    g_.wait_ge(S["cc"], 4)
    for e in evs:
        k = e["kind"]
        if k == "ln2":
            g, tb, par = e["g"], e["tb"], e["par"]
            g_.wait_ge(S["aLn"], e["aLn"])
            t0 = g * GRP + tb * 128
            g_.dma_start(out=c.xfhi_scr[t0:t0 + 128, :], in_=c.hi_stg[par][:]).then_inc(S[f"histn{par}"], 16)
            g_.dma_start(out=c.xflo_scr[t0:t0 + 128, :], in_=c.lo_stg[par][:]).then_inc(S[f"histn{par}"], 16)
            g_.dma_start(out=c.x1_scr[t0:t0 + 128, :],
                         in_=c.x_g[:, tb * 2048:(tb + 1) * 2048]).then_inc(S[f"histn{par}"], 16)
        elif k == "gate":
            g, tb = e["g"], e["tb"]
            g_.wait_ge(S["mskd"], e["mskd"])
            g_.dma_start(out=c.msk_scr[0:8, g * GRP + tb * 128:g * GRP + (tb + 1) * 128],
                         in_=c.mskT8[0:8, :]).then_inc(S["mskst"], 16)
        elif k == "fout":
            g, tb, buf = e["g"], e["tb"], e["buf"]
            g_.wait_ge(S["odone"], e["odone"])
            t0 = g * GRP + tb * 128
            g_.dma_start(out=c.out[t0:t0 + 128, :], in_=c.x1b[buf][:]).then_inc(S[f"outst{buf}"], 16)
        elif k == "p3fin":
            g_.wait_ge(S["outst0"], e["out_totals"][0])
            g_.wait_ge(S["outst1"], e["out_totals"][1])
        elif k == "prefix_fin":
            le = e["ev"]
            lk = le["kind"]
            if lk == "ln2":
                g_.wait_ge(S["histn0"], 16 * 48)
                g_.wait_ge(S["histn1"], 16 * 48)
            elif lk == "xfT":
                g_.wait_ge(S["xfl"], le["xfl"])
            elif lk == "gate":
                g_.wait_ge(S["mskst"], le["mskst"])
            elif lk == "mskb":
                g_.wait_ge(S["rin"], le["rin"])
            elif lk == "ffnh":
                g_.wait_ge(S["hdone"], le["hdone"])
            elif lk == "ffny":
                g_.wait_ge(S["tmm3"], le["tmm"])
            elif lk == "fout":
                g_.wait_ge(S[f"outst{le['buf']}"], le["out_total"])

class Ctx:
    pass


def build(cfg):
    nc = bass.Bass("TRN2", target_bir_lowering=False, debug=False)
    c = Ctx()
    c.cfg = cfg
    dram = nc.dram_tensor
    c.x = dram("x", [TW, D], F32, kind="ExternalInput").ap()
    c.wq = dram("wq", [D, 256], F32, kind="ExternalInput").ap()
    c.wk = dram("wk", [D, 256], F32, kind="ExternalInput").ap()
    c.wv = dram("wv", [D, 256], F32, kind="ExternalInput").ap()
    c.wb = dram("wb", [D, 2], F32, kind="ExternalInput").ap()
    c.convw = dram("convw", [128, 48], F32, kind="ExternalInput").ap()
    c.normw = dram("normw", [128, 256], F32, kind="ExternalInput").ap()
    c.wo = dram("wo", [D, D], F32, kind="ExternalInput").ap()
    c.gwT = dram("gwT", [D, 8], F32, kind="ExternalInput").ap()
    c.biasb = dram("biasb", [128, 8], F32, kind="ExternalInput").ap()
    c.fw13 = dram("fw13", [W13_ROWS // NC, D], F32, kind="ExternalInput").ap()
    c.fw2 = dram("fw2", [W2_ROWS // NC, D], F32, kind="ExternalInput").ap()
    c.ident = dram("ident", [128, 128], BF16, kind="ExternalInput").ap()
    c.negstril = dram("negstril", [128, 128], BF16, kind="ExternalInput").ap()
    c.triu = dram("triu", [128, 128], BF16, kind="ExternalInput").ap()
    c.ones_col = dram("ones_col", [128, 1], BF16, kind="ExternalInput").ap()
    c.ones_row = dram("ones_row", [1, 128], BF16, kind="ExternalInput").ap()
    c.coreid = dram("coreid", [1, 1], F32, kind="ExternalInput").ap()
    c.fconst = dram("fconst", [128, 8], F32, kind="ExternalInput").ap()
    c.out = dram("out", [TW, D], F32, kind="ExternalOutput").ap()
    c.ag1_in = dram("ag1_in", [TW, D], BF16)
    c.ag1_out = dram("ag1_out", [NC * TW, D], BF16)
    c.ag2_in = dram("ag2_in", [NC * TW, 256], BF16)
    c.ag2_out = dram("ag2_out", [NC * TW, 256], BF16)
    c.agw13_in = dram("agw13_in", [W13_ROWS // NC, D], BF16)
    c.agw13_out = dram("agw13_out", [W13_ROWS, D], BF16)
    c.agw2_in = dram("agw2_in", [W2_ROWS // NC, D], BF16)
    c.agw2_out = dram("agw2_out", [W2_ROWS, D], BF16)
    c.kv_scr = dram("kv_scr", [1024, T], BF16)
    c.x1_scr = dram("x1_scr", [TW, D], F32)
    c.xfhi_scr = dram("xfhi_scr", [TW, D], BF16)
    c.xflo_scr = dram("xflo_scr", [TW, D], BF16)
    c.wo_scr = dram("wo_scr", [D, D], BF16)
    c.msk_scr = dram("msk_scr", [8, TW], BF16)

    A = nc.alloc_sbuf_tensor
    c.ar_bf = A("ar_bf", [128, 69744], BF16)
    c.ar_f32 = A("ar_f32", [128, 14336], F32)
    c.xa = [c.ar_f32[:, i * 2048:(i + 1) * 2048] for i in range(2)]
    c.stats = A("stats", [128, 8], F32)
    c.xn_blk = [c.ar_bf[:, 4096 + i * 2048: 4096 + (i + 1) * 2048] for i in range(2)]
    c.identS = A("identS", [128, 128], BF16)
    c.negstrilS = A("negstrilS", [128, 128], BF16)
    c.triuS = A("triuS", [128, 128], BF16)
    c.onecS = A("onecS", [128, 1], BF16)
    c.onerS = A("onerS", [1, 128], BF16)
    c.normwS = A("normwS", [128, 256], F32)
    c.biasbS = A("biasbS", [128, 8], F32)
    c.gwhiS = A("gwhiS", [128, 16 * 8], BF16)
    c.gwloS = A("gwloS", [128, 16 * 8], BF16)
    c.wstg = [c.ar_f32[:, 4096 + i * 2048: 4096 + (i + 1) * 2048] for i in range(2)]
    c.wstgb = [c.ar_bf[:, i * 2048:(i + 1) * 2048] for i in range(2)]
    c.wqS = c.ar_bf[:, 45168:49264]
    c.wkS = c.ar_bf[:, 49264:53360]
    c.wvS = c.ar_bf[:, 53360:57456]
    c.wbS = A("wbS", [128, 16 * 2], BF16)
    c.convS = A("convS", [128, 48], F32)

    c.fconstS = A("fconstS", [128, 8], F32)
    nc.const_aps.register(F32, 1e-5, c.fconstS[:, 0:1])
    nc.const_aps.register(F32, 1e-6, c.fconstS[:, 1:2])
    nc.const_aps.register(F32, -0.5, c.fconstS[:, 2:3])
    nc.const_aps.register(F32, 0.5, c.fconstS[:, 3:4])
    nc.const_aps.register(F32, -2.772588722239781, c.fconstS[:, 4:5])
    c.sems = {}
    return nc, c


def sem(nc, c, name):
    h = nc.alloc_semaphore(name)
    c.sems[name] = h
    return h


def dve_inc(v, h, n=1):
    v.drain()
    v.engine_nop().then_inc(h, n)


def act_inc(c, sc, h, n=1):
    sc.drain()
    sc.activation(out=c.stats[:, 7:8], in_=c.fconstS[:, 0:1], func=AF.Copy).then_inc(h, n)


P0_W13_TILES = W13_ROWS // NC // 128
P0_W2_TILES = W2_ROWS // NC // 128


def p0_sync(c, s, nc):
    lda = [c.sems["p0_ld0"], c.sems["p0_ld1"]]
    cst = c.sems["p0_consts"]
    s.dma_start(out=c.identS[:], in_=c.ident).then_inc(cst, 16)
    s.dma_start(out=c.negstrilS[:], in_=c.negstril).then_inc(cst, 16)
    s.dma_start(out=c.triuS[:], in_=c.triu).then_inc(cst, 16)
    s.dma_start(out=c.onecS[:], in_=c.ones_col).then_inc(cst, 16)
    s.dma_start(out=c.onerS[:], in_=c.ones_row).then_inc(cst, 16)
    s.dma_start(out=c.normwS[:], in_=c.normw).then_inc(cst, 16)
    s.dma_start(out=c.biasbS[:], in_=c.biasb).then_inc(cst, 16)
    s.dma_start(out=c.fconstS[:], in_=c.fconst).then_inc(cst, 16)
    done = c.sems["p0_cast"]
    n = 0
    for name, src, cols, ntile in [("wq", c.wq, 256, 16), ("wk", c.wk, 256, 16),
                                   ("wv", c.wv, 256, 16), ("wb", c.wb, 2, 16),
                                   ("gw", c.gwT, 8, 16)]:
        for i in range(ntile):
            if n >= 2:
                s.wait_ge(done, n - 1)
            s.dma_start(out=c.wstg[n % 2][:, 0:cols], in_=src[i * 128:(i + 1) * 128, :]).then_inc(lda[n % 2], 16)
            n += 1
    s.dma_start(out=c.convS[:], in_=c.convw).then_inc(cst, 16)
    for i in range(P0_W13_TILES):
        if n >= 2:
            s.wait_ge(done, n - 1)
        s.dma_start(out=c.wstg[n % 2][:, 0:D], in_=c.fw13[i * 128:(i + 1) * 128, :]).then_inc(lda[n % 2], 16)
        n += 1
    for i in range(P0_W2_TILES):
        if n >= 2:
            s.wait_ge(done, n - 1)
        s.dma_start(out=c.wstg[n % 2][:, 0:D], in_=c.fw2[i * 128:(i + 1) * 128, :]).then_inc(lda[n % 2], 16)
        n += 1
    for i in range(16):
        if n >= 2:
            s.wait_ge(done, n - 1)
        s.dma_start(out=c.wstg[n % 2][:, 0:D], in_=c.wo[i * 128:(i + 1) * 128, :]).then_inc(lda[n % 2], 16)
        n += 1
    c.p0_n_wtiles = n


def p0_vector(c, v, nc):
    lda = [c.sems["p0_ld0"], c.sems["p0_ld1"]]
    done = c.sems["p0_cast"]
    n = 0
    specs = ([("wq", 256)] * 16 + [("wk", 256)] * 16 + [("wv", 256)] * 16 +
             [("wb", 2)] * 16 + [("gw", 8)] * 16)
    idx = {"wq": 0, "wk": 0, "wv": 0, "wb": 0, "wo": 0, "gw": 0}
    for name, cols in specs:
        v.wait_ge(lda[n % 2], 16 * (n // 2 + 1))
        w = c.wstg[n % 2]
        i = idx[name]
        if name == "wq":
            v.tensor_copy(out=c.wqS[:, i * 256:(i + 1) * 256], in_=w[:, 0:256])
        elif name == "wk":
            v.tensor_copy(out=c.wkS[:, i * 256:(i + 1) * 256], in_=w[:, 0:256])
        elif name == "wv":
            v.tensor_copy(out=c.wvS[:, i * 256:(i + 1) * 256], in_=w[:, 0:256])
        elif name == "wb":
            v.tensor_copy(out=c.wbS[:, i * 2:(i + 1) * 2], in_=w[:, 0:2])
        elif name == "gw":
            v.tensor_copy(out=c.gwhiS[:, i * 8:(i + 1) * 8], in_=w[:, 0:8])
            v.drain()
            v.tensor_sub(out=c.gwloS[:, i * 8:(i + 1) * 8], in0=w[:, 0:8],
                         in1=c.gwhiS[:, i * 8:(i + 1) * 8])
        idx[name] += 1
        v.drain()
        v.engine_nop().then_inc(done, 1)
        n += 1
    for i in range(P0_W13_TILES + P0_W2_TILES + 16):
        v.wait_ge(lda[n % 2], 16 * (n // 2 + 1))
        if i >= 2:
            v.wait_ge(c.sems[f"p0_st{i % 2}"], 16 * (i // 2))
        v.tensor_copy(out=c.wstgb[i % 2][:, 0:D], in_=c.wstg[n % 2][:, 0:D])
        v.drain()
        v.engine_nop().then_inc(done, 1)
        n += 1
    c.p0_cast_total = n


def p0_gpsimd(c, g, nc):
    done = c.sems["p0_cast"]
    st = [c.sems["p0_st0"], c.sems["p0_st1"]]
    ncast_w = 80
    for i in range(P0_W13_TILES + P0_W2_TILES + 16):
        g.wait_ge(done, ncast_w + i + 1)
        if i < P0_W13_TILES:
            g.dma_start(out=c.agw13_in[i * 128:(i + 1) * 128, :], in_=c.wstgb[i % 2][:, 0:D]).then_inc(st[i % 2], 16)
        elif i < P0_W13_TILES + P0_W2_TILES:
            j = i - P0_W13_TILES
            g.dma_start(out=c.agw2_in[j * 128:(j + 1) * 128, :], in_=c.wstgb[i % 2][:, 0:D]).then_inc(st[i % 2], 16)
        else:
            j = i - P0_W13_TILES - P0_W2_TILES
            g.dma_start(out=c.wo_scr[j * 128:(j + 1) * 128, :], in_=c.wstgb[i % 2][:, 0:D]).then_inc(st[i % 2], 16)
    tot = P0_W13_TILES + P0_W2_TILES + 16
    g.wait_ge(st[0], 16 * ((tot + 1) // 2))
    g.wait_ge(st[1], 16 * (tot // 2))
    cc = c.sems["cc"]
    g.collective_compute("AllGather", ALU.bypass, replica_groups=[list(range(NC))],
                         ins=[c.agw13_in.ap()], outs=[c.agw13_out.ap()]).then_inc(cc)
    g.collective_compute("AllGather", ALU.bypass, replica_groups=[list(range(NC))],
                         ins=[c.agw2_in.ap()], outs=[c.agw2_out.ap()]).then_inc(cc)


def p1_sync(c, s, nc):
    ld = [c.sems["p1_ld0"], c.sems["p1_ld1"]]
    fb = c.sems["p1_free"]
    for i in range(NBLK):
        if i >= 2:
            s.wait_ge(fb, i - 1)
        s.dma_start(out=c.xa[i % 2][:], in_=c.x[i * 128:(i + 1) * 128, :]).then_inc(ld[i % 2], 16)


def p1_vector(c, v, nc):
    ld = [c.sems["p1_ld0"], c.sems["p1_ld1"]]
    st1 = c.sems["p1_stats1"]
    dv = c.sems["p1_stats2"]
    fb = c.sems["p1_free"]
    xn = c.sems["p1_xn"]
    for i in range(NBLK):
        v.wait_ge(ld[i % 2], 16 * (i // 2 + 1))
        v.reduce_sum(out=c.stats[:, 0:1], in_=c.xa[i % 2][:], axis=AX.X)
        v.drain()
        v.tensor_scalar(out=c.stats[:, 2:3], in0=c.stats[:, 0:1], scalar1=1.0 / D,
                        scalar2=None, op0=ALU.mult)
        v.drain()
        v.tensor_mul(out=c.stats[:, 5:6], in0=c.stats[:, 2:3], in1=c.stats[:, 2:3])
        v.drain()
        v.wait_ge(c.sems["p1_sq"], i + 1)
        v.tensor_scalar(out=c.stats[:, 3:4], in0=c.stats[:, 1:2], scalar1=1.0 / D,
                        scalar2=c.stats[:, 5:6], op0=ALU.mult, op1=ALU.subtract)
        dve_inc(v, dv)
        v.wait_ge(st1, i + 1)
        if i >= 2:
            v.wait_ge(c.sems[f"p1_st{i % 2}"], 16 * (i // 2))
        v.tensor_scalar(out=c.xn_blk[i % 2][:], in0=c.xa[i % 2][:], scalar1=c.stats[:, 2:3],
                        scalar2=c.stats[:, 4:5], op0=ALU.subtract, op1=ALU.mult)
        v.drain()
        v.engine_nop().then_inc(xn, 1)
        v.engine_nop().then_inc(fb, 1)


def p1_scalar_fix(c, sc, nc):
    ld = [c.sems["p1_ld0"], c.sems["p1_ld1"]]
    sq = c.sems["p1_sq"]
    st1 = c.sems["p1_stats1"]
    dv = c.sems["p1_stats2"]
    sc.wait_ge(c.sems["p0_consts"], 16 * 8)
    tot = P0_W13_TILES + P0_W2_TILES + 16
    sc.wait_ge(c.sems["p0_st0"], 16 * ((tot + 1) // 2))
    sc.wait_ge(c.sems["p0_st1"], 16 * (tot // 2))
    for i in range(NBLK):
        sc.wait_ge(ld[i % 2], 16 * (i // 2 + 1))
        sc.activation(out=c.wstgb[0][:, 0:D], in_=c.xa[i % 2][:], func=AF.Square,
                      accum_out=c.stats[:, 1:2])
        act_inc(c, sc, sq)
        sc.wait_ge(dv, i + 1)
        sc.activation(out=c.stats[:, 6:7], in_=c.stats[:, 3:4], func=AF.Ln, bias=1e-5)
        sc.drain()
        sc.activation(out=c.stats[:, 4:5], in_=c.stats[:, 6:7], func=AF.Exp, scale=-0.5)
        act_inc(c, sc, st1)


def p1_gpsimd(c, g, nc):
    xn = c.sems["p1_xn"]
    st = [c.sems["p1_st0"], c.sems["p1_st1"]]
    cc = c.sems["cc"]
    for i in range(NBLK):
        g.wait_ge(xn, i + 1)
        g.dma_start(out=c.ag1_in[i * 128:(i + 1) * 128, :], in_=c.xn_blk[i % 2][:]).then_inc(st[i % 2], 16)
    g.wait_ge(st[0], 16 * (NBLK // 2))
    g.wait_ge(st[1], 16 * (NBLK // 2))
    g.collective_compute("AllGather", ALU.bypass,
                         replica_groups=[list(range(NC))],
                         ins=[c.ag1_in.ap()], outs=[c.ag1_out.ap()]).then_inc(cc)


def host_constants():
    consts = {
        "fconst": np.tile(np.array([[1e-5, 1e-6, -0.5, 0.5, -2.772588722239781, 0, 0, 0]], np.float32), (128, 1)),
        "ident": np.eye(128, dtype=BF),
        "negstril": (-np.tril(np.ones((128, 128), np.float32), -1)).astype(BF),
        "triu": np.triu(np.ones((128, 128), np.float32)).astype(BF),
        "ones_col": np.ones((128, 1), BF),
        "ones_row": np.ones((1, 128), BF),
        "coreid": np.zeros((1, 1), np.float32),
    }
    return consts


def host_in_maps(inputs):
    x = np.ascontiguousarray(inputs["L0_attn_hidden_states"].reshape(B * T, D))
    Wq, Wk, Wv = inputs["L0_attn_Wq"], inputs["L0_attn_Wk"], inputs["L0_attn_Wv"]
    Wb, Wo = inputs["L0_attn_Wb"], inputs["L0_attn_Wo"]
    cq, ck, cv = inputs["L0_attn_conv_q"], inputs["L0_attn_conv_k"], inputs["L0_attn_conv_v"]
    normw = inputs["L0_attn_norm_w"]
    gw = inputs["L0_ffn_gate_w"]
    ebias = inputs["L0_ffn_expert_bias"]
    w1, w2, w3 = inputs["L0_ffn_w1"], inputs["L0_ffn_w2"], inputs["L0_ffn_w3"]
    sw1, sw2, sw3 = inputs["L0_ffn_sw1"], inputs["L0_ffn_sw2"], inputs["L0_ffn_sw3"]
    def hbmajor(MT):
        A4 = MT.reshape(16, 128, 8, 128)
        return np.ascontiguousarray(A4.transpose(2, 1, 0, 3).reshape(8 * 128, 16 * 128))
    blocks13 = []
    for e in range(E):
        blocks13.append(hbmajor(np.ascontiguousarray(w1[e].T)))
        blocks13.append(hbmajor(np.ascontiguousarray(w3[e].T)))
    blocks13.append(hbmajor(np.ascontiguousarray(sw1.T)))
    blocks13.append(hbmajor(np.ascontiguousarray(sw3.T)))
    img13 = np.concatenate(blocks13, axis=0)
    blocks2 = [np.ascontiguousarray(w2[e].T) for e in range(E)] + [np.ascontiguousarray(sw2.T)]
    img2 = np.concatenate(blocks2, axis=0)
    consts = host_constants()
    in_maps = []
    for cid in range(NC):
        b = cid // 4
        t0 = (cid % 4) * TW
        hc = cid
        colsl = slice(hc * 256, hc * 256 + 256)
        convw = np.zeros((128, 48), np.float32)
        for pi, cw in enumerate([cq, ck, cv]):
            sl = cw[colsl, :]
            for blk in range(4):
                mb = blk % 2
                convw[:, pi * 16 + blk * 4:pi * 16 + blk * 4 + 4] = sl[mb * 128:(mb + 1) * 128, :]
        wb2 = np.concatenate([Wb[:, hc:hc + 1], Wb[:, hc:hc + 1]], axis=1)
        m = {
            "x": np.ascontiguousarray(x[b * T + t0: b * T + t0 + TW]),
            "wq": np.ascontiguousarray(Wq[:, colsl]),
            "wk": np.ascontiguousarray(Wk[:, colsl]),
            "wv": np.ascontiguousarray(Wv[:, colsl]),
            "wb": np.ascontiguousarray(wb2),
            "convw": convw,
            "normw": np.ascontiguousarray(np.tile(np.asarray(normw)[None, :], (128, 1))),
            "wo": np.ascontiguousarray(Wo),
            "gwT": np.ascontiguousarray(gw.T),
            "biasb": np.ascontiguousarray(np.tile(np.asarray(ebias)[None, :], (128, 1))),
            "fw13": np.ascontiguousarray(img13[cid * (W13_ROWS // NC):(cid + 1) * (W13_ROWS // NC)]),
            "fw2": np.ascontiguousarray(img2[cid * (W2_ROWS // NC):(cid + 1) * (W2_ROWS // NC)]),
        }
        m.update(consts)
        in_maps.append(m)
    return in_maps


def build_full(p3_mode="full"):
    cfg = {}
    nc, c = build(cfg)
    p2_alloc(nc, c)
    p3_alloc(nc, c)
    ev2, _ = build_p2_events()
    ev3, _ = build_p3_events()
    dbgA = None
    if p3_mode != "full":
        stop_kind, nth = {"passA": ("ln2", 32), "xft": ("xfT", 1),
                          "gate": ("gate", 4), "gate1": ("gate", 1), "mskb": ("mskb", 4),
                          "ffnh": ("ffnh", 8), "ffny": ("ffny", 4),
                          "fout": ("fout", 4)}[p3_mode]
        cnt_k = 0
        idx = None
        for i, e in enumerate(ev3):
            if e["kind"] == stop_kind:
                cnt_k += 1
                if cnt_k == nth:
                    idx = i
                    break
        assert idx is not None
        last = ev3[idx]
        fin = {"kind": "prefix_fin", "ev": last}
        ev3 = ev3[:idx + 1] + [fin]
        if p3_mode == "passA":
            dbgA = (nc.dram_tensor("dbg_x1", [TW, D], F32, kind="ExternalOutput").ap(),
                    nc.dram_tensor("dbg_hi", [TW, D], BF16, kind="ExternalOutput").ap(),
                    nc.dram_tensor("dbg_wo", [D, D], BF16, kind="ExternalOutput").ap(),
                    nc.dram_tensor("dbg_ag2", [2048, 512], BF16, kind="ExternalOutput").ap())
            c.dbgA = dbgA
        if p3_mode == "gate":
            dbgA = ("gate", nc.dram_tensor("dbg_msk", [8, TW], BF16, kind="ExternalOutput").ap())
            c.dbgA = dbgA
    names = (["p0_ld0", "p0_ld1", "p0_consts", "p0_cast", "p0_st0", "p0_st1",
              "cc", "p1_ld0", "p1_ld1", "p1_free", "p1_stats1", "p1_stats2", "p1_sq",
              "p1_xn", "p1_st0", "p1_st1"] + P2_SEM_NAMES + P3_SEM_NAMES)
    groups8 = [list(range(NC))]
    with nc.Block() as block:
        for name in names:
            sem(nc, c, name)

        @block.sync
        def _(s_):
            p0_sync(c, s_, nc)
            p1_sync(c, s_, nc)
            s_.wait_ge(c.sems["cc"], 3)
            p2_sync(c, s_, nc, ev2)
            p3_sync(c, s_, nc, ev3)

        @block.vector
        def _(v):
            p0_vector(c, v, nc)
            p1_vector(c, v, nc)
            p2_vector(c, v, nc, ev2, dve_inc)
            p3_vector(c, v, nc, ev3, dve_inc)

        @block.scalar
        def _(sc):
            p1_scalar_fix(c, sc, nc)
            p2_scalar(c, sc, nc, ev2, act_inc)
            p3_scalar(c, sc, nc, ev3, act_inc)

        @block.tensor
        def _(t):
            p2_tensor(c, t, nc, ev2)
            p3_tensor(c, t, nc, ev3)

        @block.gpsimd
        def _(g):
            p0_gpsimd(c, g, nc)
            p1_gpsimd(c, g, nc)
            p2_gpsimd(c, g, nc, ev2, groups8)
            p3_gpsimd(c, g, nc, ev3)
            if dbgA is not None and dbgA[0] == "gate":
                fin = c.sems["odone"]
                g.dma_start(out=dbgA[1], in_=c.msk_scr.ap()).then_inc(fin, 16)
                g.wait_ge(fin, 16)
            elif dbgA is not None:
                fin = c.sems["odone"]
                g.dma_start(out=dbgA[0], in_=c.x1_scr.ap()).then_inc(fin, 16)
                g.dma_start(out=dbgA[1], in_=c.xfhi_scr.ap()).then_inc(fin, 16)
                g.dma_start(out=dbgA[2], in_=c.wo_scr.ap()).then_inc(fin, 16)
                for r in range(4):
                    g.dma_start(out=dbgA[3][r * 512:(r + 1) * 512, :],
                                in_=c.ag2_out[r * 4096:r * 4096 + 512, :]).then_inc(fin, 16)
                g.wait_ge(fin, 112)
    return nc


def kernel(**inputs):
    from concourse.bass_utils import run_bass_kernel_spmd
    in_maps = host_in_maps(inputs)
    nc = build_full()
    res = run_bass_kernel_spmd(nc, in_maps, core_ids=list(range(NC)))
    out = np.zeros((B * T, D), np.float32)
    for cid in range(NC):
        b = cid // 4
        t0 = (cid % 4) * TW
        out[b * T + t0: b * T + t0 + TW] = res.results[cid]["out"]
    return out.reshape(B, T, D)
